# version 02
# baseline (speedup 1.0000x reference)
import contextlib
import math
import numpy as np
import ml_dtypes
import concourse.bass as bass
import concourse.mybir as mybir
from concourse.bass_utils import run_bass_kernel_spmd

F32 = mybir.dt.float32
BF16 = mybir.dt.bfloat16
AF = mybir.ActivationFunctionType
ALU = mybir.AluOpType
AXX = mybir.AxisListType.X

SC = 512
EPS = 1e-5


class CFG:
    def __init__(self, D=2048, TOK=4096, OWN=2048, H=16, E=32, INTER=768,
                 CH=16, CI=128, NQ=4):
        self.D, self.TOK, self.OWN, self.H = D, TOK, OWN, H
        self.DH = 128
        self.E, self.INTER, self.CH, self.CI, self.NQ = E, INTER, CH, CI, NQ
        self.KT = D // 128
        self.QT = OWN // 128
        self.TT = OWN // 128
        self.KCH = TOK // SC
        self.OCH = OWN // SC
        self.EXPC = min(1024, OWN)
        self.NEX = TOK // self.EXPC
        self.IT = INTER // 128
        self.CIT = CI // 128
        self.QL = OWN // NQ
        self.R = self.KCH * H
        assert TOK == 2 * OWN and self.QL <= SC and OWN % SC == 0
        assert self.R <= 128 and D == H * self.DH and self.QL % 128 == 0


FULL = CFG()


def mk_consts(cfg):
    c = {}
    c["idf"] = np.eye(128, dtype=np.float32)
    oh = np.zeros((128, 8 * 8), dtype=np.float32)
    for ch in range(8):
        oh[:, 8 * ch + ch] = 1.0 / cfg.D
    c["oh8"] = oh
    s8 = np.zeros((8, 8 * 128), dtype=np.float32)
    for r in range(8):
        s8[r, 128 * r:128 * (r + 1)] = 1.0
    c["sel8"] = s8
    c["ones128"] = np.ones((128, 1), dtype=np.float32)
    c["epsb"] = np.full((128, 1), 1e-5, dtype=np.float32)
    sE = np.zeros((cfg.E, cfg.E * 128), dtype=ml_dtypes.bfloat16)
    for e in range(cfg.E):
        sE[e, 128 * e:128 * (e + 1)] = 1.0
    c["selE"] = sE
    sC = np.zeros((cfg.CH, cfg.CH * 128), dtype=ml_dtypes.bfloat16)
    for e in range(cfg.CH):
        sC[e, 128 * e:128 * (e + 1)] = 1.0
    c["selC"] = sC
    return c


def mk_selR(cfg, half):
    R, H, OCH = cfg.R, cfg.H, cfg.OCH
    s0 = np.zeros((R, OCH * H * 128), dtype=ml_dtypes.bfloat16)
    s1 = np.zeros((R, OCH * H * 128), dtype=ml_dtypes.bfloat16)
    for i in range(OCH):
        for h in range(H):
            blk = i * H + h
            mine = s0 if half == 0 else s1
            peer = s1 if half == 0 else s0
            mine[i * H + h, 128 * blk:128 * (blk + 1)] = 1.0
            peer[(OCH + i) * H + h, 128 * blk:128 * (blk + 1)] = 1.0
    return s0, s1


class Builder:
    def __init__(self, cfg, dump=None):
        self.cfg = cfg
        self.dump = dump
        self.nc = bass.Bass("TRN2", target_bir_lowering=False, debug=False)
        self.streams = {k: [] for k in "TSVPY"}
        self.cnt = {}
        self.sems = {}
        self.es = contextlib.ExitStack()
        for s in ("dC", "dD", "dG", "dE", "x0", "x1", "wq0", "wq1", "st0", "st1", "wv0", "wv1",
                  "cs0", "wo0", "wo1", "xr0", "xr1", "xo0", "xo1",
                  "xq0", "xq1", "ao", "Lg", "Lu", "Ld", "Sg", "Su", "Sd",
                  "sT", "sS", "sV", "sP", "cc"):
            self.sems[s] = self.es.enter_context(self.nc.semaphore(s))
            self.cnt[s] = 0
        self.dumps = {}

    def w(self, eng, s, val):
        if val > 0:
            self.streams[eng].append(lambda e, ss=self.sems[s], v=val: e.wait_ge(ss, v))

    def op(self, eng, fn, inc=None, n=1):
        if inc is None:
            self.streams[eng].append(fn)
            return None
        self.cnt[inc] += n
        self.streams[eng].append(
            lambda e, fn=fn, ss=self.sems[inc], nn=n: fn(e).then_inc(ss, nn))
        return self.cnt[inc]

    def barrier(self):
        snap = dict(self.cnt)
        for eng in "TSVPY":
            for s, v in snap.items():
                self.w(eng, s, v)

    def sbuf(self, scp, name, shape, dt, side="right"):
        return scp.enter_context(self.nc.sbuf_tensor("sb_" + name, shape, dt, side=side))

    def psum(self, scp, name, shape, dt=F32):
        return scp.enter_context(self.nc.psum_tensor("ps_" + name, shape, dt))

    def dmp(self, name, shape, dt=F32):
        self.dumps[name] = self.nc.dram_tensor("dbg_" + name, shape, dt,
                                               kind="ExternalOutput").ap()
        return self.dumps[name]

    def dump_tiled(self, name, t):
        d = self.dumps[name]
        self.op("P", lambda e, d=d, t=t: e.dma_start(
            out=d.rearrange("(a p) f -> p a f", p=128), in_=t[:]), "dD", 16)

    def dump_flat(self, name, t):
        d = self.dumps[name]
        self.op("P", lambda e, d=d, t=t: e.dma_start(out=d, in_=t[:]), "dD", 16)

    def declare_io(self):
        nc, cfg = self.nc, self.cfg
        D, TOK, H, E, CH, CI, KT, OWN = (cfg.D, cfg.TOK, cfg.H, cfg.E, cfg.CH,
                                         cfg.CI, cfg.KT, cfg.OWN)
        ap = lambda *a, **k: nc.dram_tensor(*a, **k).ap()
        self.xT = ap("xT", [D, TOK], F32, kind="ExternalInput")
        self.wqkh = ap("wqkh", [H, KT, 128, 384], F32, kind="ExternalInput")
        self.wo_d = ap("wo", [D, D], F32, kind="ExternalInput")
        self.gwT_d = ap("gwT", [D, E], F32, kind="ExternalInput")
        self.wg_d = ap("wg", [E, D, cfg.INTER], F32, kind="ExternalInput")
        self.wu_d = ap("wu", [E, D, cfg.INTER], F32, kind="ExternalInput")
        self.wd_d = ap("wd", [E, cfg.INTER, D], F32, kind="ExternalInput")
        self.cwg_d = ap("cwg", [CH, D, CI], F32, kind="ExternalInput")
        self.cwu_d = ap("cwu", [CH, D, CI], F32, kind="ExternalInput")
        self.cwd_d = ap("cwd", [CH, CI, D], F32, kind="ExternalInput")
        self.idf_d = ap("idf", [128, 128], F32, kind="ExternalInput")
        self.oh8_d = ap("oh8", [128, 64], F32, kind="ExternalInput")
        self.sel8_d = ap("sel8", [8, 1024], F32, kind="ExternalInput")
        self.ones_d = ap("ones128", [128, 1], F32, kind="ExternalInput")
        self.epsb_d = ap("epsb", [128, 1], F32, kind="ExternalInput")
        self.selR0_d = ap("selR0", [cfg.R, cfg.OCH * H * 128], BF16, kind="ExternalInput")
        self.selR1_d = ap("selR1", [cfg.R, cfg.OCH * H * 128], BF16, kind="ExternalInput")
        self.selE_d = ap("selE", [E, E * 128], BF16, kind="ExternalInput")
        self.selC_d = ap("selC", [CH, CH * 128], BF16, kind="ExternalInput")
        self.out_d = ap("out", [D, OWN], F32, kind="ExternalOutput")
        self.xnew_d = ap("xnew", [D, OWN], F32)
        self.cc_in = nc.dram_tensor("cc_in", [cfg.R, SC], F32)
        self.cc_out = nc.dram_tensor("cc_out", [2 * cfg.R, SC], F32)

    def load_consts(self):
        s = self.es
        self.idf = self.sbuf(s, "idf", [128, 128], F32, side="left")
        self.oh8 = self.sbuf(s, "oh8", [128, 64], F32, side="left")
        self.sel8 = self.sbuf(s, "sel8", [8, 1024], F32, side="left")
        self.ones128 = self.sbuf(s, "ones", [128, 1], F32, side="left")
        self.epsb = self.sbuf(s, "epsb", [128, 1], F32, side="left")
        cfg = self.cfg
        self.nm2_row = self.sbuf(s, "nm2_row", [8, SC], F32, side="left")
        self.rstd2_row = self.sbuf(s, "rstd2_row", [8, SC], F32, side="left")
        self.r2w = self.sbuf(s, "r2w", [8, SC], F32, side="left")
        self.nm2T = self.sbuf(s, "nm2T", [128, 4, 8], F32, side="left")
        self.rstd2T = self.sbuf(s, "rstd2T", [128, 4, 8], F32, side="left")
        self.l0T = self.sbuf(s, "l0T", [128, cfg.TT, cfg.E], F32, side="left")
        self.gsum_b = self.sbuf(s, "gsum_b", [128, cfg.E], F32, side="left")
        self.gwTb = self.sbuf(s, "gwTb", [128, cfg.KT, cfg.E], F32, side="left")
        self.gsrow = self.sbuf(s, "gsrow", [1, cfg.E], F32, side="left")
        self.cwlT = self.sbuf(s, "cwlT", [cfg.E, cfg.OWN], BF16, side="left")
        self.cwsT = self.sbuf(s, "cwsT", [cfg.CH, cfg.OWN], BF16, side="left")
        for dst, src in ((self.idf, self.idf_d), (self.oh8, self.oh8_d),
                         (self.sel8, self.sel8_d), (self.ones128, self.ones_d),
                         (self.epsb, self.epsb_d)):
            self.op("Y", lambda e, d=dst, sr=src: e.dma_start(out=d[:], in_=sr), "dC", 16)
        v = self.cnt["dC"]
        for eng in "TSVP":
            self.w(eng, "dC", v)

    def ph1_ln1(self):
        cfg = self.cfg
        KT, TOK, OWN, EXPC, KCH = cfg.KT, cfg.TOK, cfg.OWN, cfg.EXPC, cfg.KCH
        w, op = self.w, self.op
        self.ln1_own_scope = contextlib.ExitStack()
        self.ln1_oth_scope = contextlib.ExitStack()
        self.ln1_own = self.sbuf(self.ln1_own_scope, "ln1_own", [128, KT, OWN], BF16, side="left")
        self.ln1_oth = self.sbuf(self.ln1_oth_scope, "ln1_oth", [128, KT, OWN], BF16, side="left")
        with contextlib.ExitStack() as ph:
            xs = [self.sbuf(ph, f"xs{i}", [128, EXPC], F32) for i in range(2)]
            xsq = [self.sbuf(ph, f"xsq{i}", [128, EXPC], F32) for i in range(2)]
            rows = self.sbuf(ph, "rows", [8, SC * 4], F32)
            nmrb = self.sbuf(ph, "nmrb", [128, EXPC], F32)
            rstdb = self.sbuf(ph, "rstdb", [128, EXPC], F32)
            mean_ps = self.psum(ph, "mean_ps", [8, SC])
            msq_ps = self.psum(ph, "msq_ps", [8, SC])
            bc_ps = [self.psum(ph, f"bc_ps{i}", [128, SC]) for i in range(2)]

            nch = TOK // EXPC
            total = KT * nch
            xs_free = {}
            i = 0
            for kt in range(KT):
                for chh in range(nch):
                    par = i % 2
                    if i >= 2:
                        w("Y", "sT", xs_free[i - 2])
                    dy = op("Y", lambda e, p=par, k=kt, c=chh:
                            e.dma_start(out=xs[p][:],
                                        in_=self.xT[128 * k:128 * (k + 1), EXPC * c:EXPC * (c + 1)]),
                            f"x{par}", 16)
                    w("S", f"x{par}", dy)
                    ss = op("S", lambda e, p=par:
                            e.activation(out=xsq[p][:], in_=xs[p][:], func=AF.Square), "sS")
                    w("T", f"x{par}", dy)
                    w("T", "sS", ss)
                    for j in range(EXPC // SC):
                        c512 = chh * (EXPC // SC) + j
                        first = (i == 0 and j == 0)
                        last = (i == total - 1 and j == (EXPC // SC) - 1)
                        op("T", lambda e, p=par, c=c512, f=first, l=last, jj=j:
                           e.matmul(mean_ps[:], self.oh8[:, 8 * c:8 * (c + 1)],
                                    xs[p][:, SC * jj:SC * (jj + 1)], start=f, stop=l), None)
                        tl = op("T", lambda e, p=par, c=c512, f=first, l=last, jj=j:
                                e.matmul(msq_ps[:], self.oh8[:, 8 * c:8 * (c + 1)],
                                         xsq[p][:, SC * jj:SC * (jj + 1)], start=f, stop=l), "sT")
                    xs_free[i] = tl
                    i += 1
            stats_T = self.cnt["sT"]
            sS_total = self.cnt["sS"] + 1
            w("V", "sT", stats_T)
            op("V", lambda e: e.tensor_copy(rows[:, 0:SC], mean_ps[:]), "sV")
            vv = op("V", lambda e: e.tensor_copy(rows[:, SC:2 * SC], msq_ps[:]), "sV")
            w("V", "sV", vv)
            vv = op("V", lambda e: e.tensor_mul(rows[:, 2 * SC:3 * SC], rows[:, 0:SC], rows[:, 0:SC]), "sV")
            w("V", "sV", vv)
            vv = op("V", lambda e: e.tensor_sub(rows[:, SC:2 * SC], rows[:, SC:2 * SC],
                                                rows[:, 2 * SC:3 * SC]), "sV")
            w("S", "sV", vv)
            ssq = op("S", lambda e: e.activation(out=rows[:, 2 * SC:3 * SC], in_=rows[:, SC:2 * SC],
                                                 func=AF.Sqrt, bias=self.epsb[0:8, :]), "sS")
            w("V", "sS", ssq)
            vv = op("V", lambda e: e.reciprocal(rows[:, 2 * SC:3 * SC], rows[:, 2 * SC:3 * SC]), "sV")
            w("V", "sV", vv)
            op("V", lambda e: e.scalar_tensor_tensor(rows[:, 3 * SC:4 * SC], rows[:, 0:SC], -1.0,
                                                     rows[:, 2 * SC:3 * SC], ALU.mult, ALU.mult), "sV")
            rows_V = self.cnt["sV"]
            i = 0
            ap_free = {}
            xs_freeV = {}
            bc_P = 0
            for chh in range(nch):
                bc_free = 0
                for j in range(EXPC // SC):
                    c512 = chh * (EXPC // SC) + j
                    for roff, dst in ((3 * SC, nmrb), (2 * SC, rstdb)):
                        par = (2 * j + (0 if roff == 3 * SC else 1)) % 2
                        w("T", "sV", rows_V)
                        w("T", "sV", bc_free)
                        tb = op("T", lambda e, cc=c512, ro=roff, p=par:
                                e.matmul(bc_ps[p][:], self.sel8[:, 128 * cc:128 * (cc + 1)],
                                         rows[:, ro:ro + SC], start=True, stop=True), "sT")
                        w("V", "sT", tb)
                        w("V", "sP", bc_P)
                        bc_free = op("V", lambda e, d=dst, jj=j, p=par:
                                     e.tensor_copy(d[:, SC * jj:SC * (jj + 1)], bc_ps[p][:]), "sV")
                bc_done = self.cnt["sV"]
                g0 = chh * EXPC
                if g0 < OWN:
                    tdst, loc = self.ln1_own, g0
                else:
                    tdst, loc = self.ln1_oth, g0 - OWN
                for kt in range(KT):
                    par = i % 2
                    w("Y", "sP", ap_free.get(i - 2, 0))
                    w("Y", "sV", xs_freeV.get(i - 2, 0))
                    if i < 2:
                        w("Y", "sT", stats_T)
                    dy = op("Y", lambda e, p=par, k=kt, c=chh:
                            e.dma_start(out=xs[p][:],
                                        in_=self.xT[128 * k:128 * (k + 1), EXPC * c:EXPC * (c + 1)]),
                            f"x{par}", 16)
                    w("V", f"x{par}", dy)
                    if kt == 0:
                        w("V", "sV", bc_done)
                    if i >= 2:
                        w("V", "sP", ap_free[i - 2])
                    w("V", "sS", sS_total)
                    vv = op("V", lambda e, p=par:
                            e.tensor_mul(xsq[p][:], xs[p][:], rstdb[:]), "sV")
                    xs_freeV[i] = vv
                    w("P", "sV", vv)
                    ap_free[i] = op("P", lambda e, p=par, k=kt, td=tdst, lc=loc:
                                    e.tensor_add(td[:, k, lc:lc + EXPC], xsq[p][:],
                                                 nmrb[:]), "sP")
                    i += 1
                bc_P = self.cnt["sP"]
            self.barrier()
        self.barrier()
        if self.dump == "ln1":
            self.dmp("ln1o", [self.cfg.KT * 128, OWN])
            self.dmp("ln1x", [self.cfg.KT * 128, OWN])
            self.dump_tiled("ln1o", self.ln1_own)
            self.dump_tiled("ln1x", self.ln1_oth)

    def ph2_attn(self):
        cfg = self.cfg
        KT, OWN, TOK, H, QT = cfg.KT, cfg.OWN, cfg.TOK, cfg.H, cfg.QT
        KCH, OCH, NEX, EXPC = cfg.KCH, cfg.OCH, cfg.NEX, cfg.EXPC
        DHSC = 1.0 / math.sqrt(cfg.DH)
        w, op = self.w, self.op

        def ln1_src(c):
            if c < OCH:
                return self.ln1_own, c * SC
            return self.ln1_oth, (c - OCH) * SC

        with contextlib.ExitStack() as ph:
            wqk = [self.sbuf(ph, f"wqk{i}", [128, KT, 256], BF16) for i in range(2)]
            qh = self.sbuf(ph, "qh", [128, OWN], BF16)
            kh = self.sbuf(ph, "kh", [128, TOK], BF16)
            exp_sb = [self.sbuf(ph, f"exp{i}", [128, TOK], BF16) for i in range(2)]
            zw = self.sbuf(ph, "zw", [128, 2 * (NEX + 2)], F32)
            rzb = [self.sbuf(ph, f"rzb{i}", [128, 1], BF16) for i in range(2)]
            stg = [self.sbuf(ph, f"stg{i}", [1, SC], F32) for i in range(2)]
            ncolb = (KCH + 2) // 3
            sc_ps = [self.psum(ph, f"sc_ps{i}", [128, EXPC]) for i in range(2)]
            qk_ps = self.psum(ph, "qk_ps", [128, SC])
            col_ps = [self.psum(ph, f"col_ps{i}", [128, SC]) for i in range(ncolb)]

            wq_dma = [0, 0]
            wq_freeT = [0, 0]
            qk_evac = 0
            scps_free = [0, 0]
            ebuf_free = [0, 0]
            col_evacV = 0
            stg_dma = [0, 0]
            prev_scoresT = 0

            for h in range(H):
                slot = h % 2
                w("P", "sT", wq_freeT[slot])
                wq_dma[slot] = op("P", lambda e, hh=h, s=slot:
                                  e.dma_start(out=wqk[s][:],
                                              in_=self.wqkh[hh].rearrange("a p f -> p a f")[:, :, 0:256]),
                                  f"wq{slot}", 16)
                qkv_specs = (
                    [(0, self.ln1_own, c * SC, "q", c * SC) for c in range(OCH)] +
                    [(128,) + ln1_src(c) + ("k", c * SC) for c in range(KCH)]
                )
                firstspec = True
                for (woff, srcT, srcloc, which, dstloc) in qkv_specs:
                    w("T", f"wq{slot}", wq_dma[slot])
                    w("T", "sV", qk_evac)
                    for kt in range(KT):
                        tc = op("T", lambda e, s=slot, k=kt, wf=woff, st=srcT, sl=srcloc:
                                e.matmul(qk_ps[:], wqk[s][:, k, wf:wf + 128],
                                         st[:, k, sl:sl + SC],
                                         start=(k == 0), stop=(k == KT - 1)),
                                "sT" if kt == KT - 1 else None)
                    w("V", "sT", tc)
                    if firstspec:
                        w("V", "sT", prev_scoresT)
                        firstspec = False
                    if which == "q":
                        qk_evac = op("V", lambda e, dl=dstloc:
                                     e.tensor_copy(qh[:, dl:dl + SC], qk_ps[:]), "sV")
                    else:
                        qk_evac = op("V", lambda e, dl=dstloc:
                                     e.tensor_copy(kh[:, dl:dl + SC], qk_ps[:]), "sV")
                wq_freeT[slot] = self.cnt["sT"]
                qkv_done = qk_evac

                colmm = {}
                rzv_q = {}

                def emit_colsum(q):
                    epar2 = q % 2
                    w("T", "sV", rzv_q[q])
                    if q == 0:
                        w("T", "sV", col_evacV)
                    for c in range(KCH):
                        bank, row = c // 3, 32 * (c % 3)
                        tc2 = op("T", lambda e, q2=q, b=bank, r=row, c2=c, ep=epar2:
                                 e.matmul(col_ps[b][r:r + 1, :], rzb[ep][:],
                                          exp_sb[ep][:, SC * c2:SC * (c2 + 1)],
                                          start=(q2 == 0), stop=(q2 == QT - 1)),
                                 "sT" if c == KCH - 1 else None)
                    ebuf_free[epar2] = tc2
                    colmm[q] = tc2

                for qt in range(QT):
                    epar = qt % 2
                    zc = (NEX + 2) * epar
                    last_exp = 0
                    for ch2 in range(NEX):
                        par = ch2 % 2
                        w("T", "sS", scps_free[par])
                        if ch2 == 0 and qt == 0:
                            w("T", "sV", qkv_done)
                        for j in range(EXPC // SC):
                            c512 = ch2 * (EXPC // SC) + j
                            tc = op("T", lambda e, q=qt, p=par, c=c512, jj=j:
                                    e.matmul(sc_ps[p][:, SC * jj:SC * (jj + 1)],
                                             qh[:, 128 * q:128 * (q + 1)],
                                             kh[:, SC * c:SC * (c + 1)],
                                             start=True, stop=True),
                                    "sT" if j == (EXPC // SC) - 1 else None)
                        w("S", "sT", tc)
                        w("S", "sT", ebuf_free[epar])
                        scps_free[par] = op(
                            "S", lambda e, p=par, c=ch2, ep=epar, z=zc:
                            e.activation(out=exp_sb[ep][:, EXPC * c:EXPC * (c + 1)],
                                         in_=sc_ps[p][:], func=AF.Exp, scale=DHSC,
                                         accum_out=zw[:, z + c:z + c + 1]), "sS")
                        last_exp = scps_free[par]
                    w("V", "sS", last_exp)
                    w("V", "sT", ebuf_free[epar])
                    vv = op("V", lambda e, z=zc: e.reduce_sum(zw[:, z + NEX:z + NEX + 1],
                                                              zw[:, z:z + NEX], axis=AXX), "sV")
                    w("V", "sV", vv)
                    vv = op("V", lambda e, z=zc: e.reciprocal(zw[:, z + NEX + 1:z + NEX + 2],
                                                              zw[:, z + NEX:z + NEX + 1]), "sV")
                    w("V", "sV", vv)
                    rzv_q[qt] = op("V", lambda e, z=zc, ep=epar:
                                   e.tensor_copy(rzb[ep][:], zw[:, z + NEX + 1:z + NEX + 2]), "sV")
                    if qt > 0:
                        emit_colsum(qt - 1)
                emit_colsum(QT - 1)
                prev_scoresT = self.cnt["sT"]
                w("V", "sT", colmm[QT - 1])
                for c in range(KCH):
                    bank, row = c // 3, 32 * (c % 3)
                    spar = c % 2
                    w("V", f"st{spar}", stg_dma[spar])
                    ev = op("V", lambda e, b=bank, r=row, sp=spar:
                            e.tensor_copy(stg[sp][:], col_ps[b][r:r + 1, :]), "sV")
                    w("S", "sV", ev)
                    stg_dma[spar] = op("S", lambda e, sp=spar, rr=c * H + h:
                                       e.dma_start(out=self.cc_in.ap()[rr:rr + 1, :],
                                                   in_=stg[sp][:]), f"st{spar}", 16)
                col_evacV = self.cnt["sV"]
            self.attn_dS = (self.cnt["st0"], self.cnt["st1"])
            self.barrier()
        self.barrier()
        self.ln1_oth_scope.close()

    def ph2b_vproj(self):
        cfg = self.cfg
        KT, OWN, H, OCH = cfg.KT, cfg.OWN, cfg.H, cfg.OCH
        w, op = self.w, self.op
        self.vsb_scope = contextlib.ExitStack()
        self.v_sb = self.sbuf(self.vsb_scope, "v_sb", [128, H, OWN], BF16)
        with contextlib.ExitStack() as ph:
            wv = [self.sbuf(ph, f"wv{i}", [128, KT, 128], BF16) for i in range(2)]
            v_ps = [self.psum(ph, f"v_ps{i}", [128, SC]) for i in range(2)]
            wv_dma = [0, 0]
            wv_freeT = [0, 0]
            vps_freeV = [0, 0]
            for h in range(H):
                slot = h % 2
                w("P", "sT", wv_freeT[slot])
                wv_dma[slot] = op("P", lambda e, hh=h, s=slot:
                                  e.dma_start(out=wv[s][:],
                                              in_=self.wqkh[hh].rearrange("a p f -> p a f")[:, :, 256:384]),
                                  f"wv{slot}", 16)
                for c in range(OCH):
                    par = (h * OCH + c) % 2
                    w("T", f"wv{slot}", wv_dma[slot])
                    w("T", "sV", vps_freeV[par])
                    for kt in range(KT):
                        tc = op("T", lambda e, s=slot, k=kt, c2=c, p=par:
                                e.matmul(v_ps[p][:], wv[s][:, k, :],
                                         self.ln1_own[:, k, SC * c2:SC * (c2 + 1)],
                                         start=(k == 0), stop=(k == KT - 1)),
                                "sT" if kt == KT - 1 else None)
                    w("V", "sT", tc)
                    vps_freeV[par] = op("V", lambda e, hh=h, c2=c, p=par:
                                        e.tensor_copy(self.v_sb[:, hh, SC * c2:SC * (c2 + 1)],
                                                      v_ps[p][:]), "sV")
                wv_freeT[slot] = self.cnt["sT"]
            self.barrier()
        self.barrier()
        self.ln1_own_scope.close()

    def ph3_val_wo(self):
        cfg = self.cfg
        KT, OWN, H, OCH, E = cfg.KT, cfg.OWN, cfg.H, cfg.OCH, cfg.E
        w, op = self.w, self.op
        v_sb = self.v_sb
        self.valT_scope = contextlib.ExitStack()
        valT = self.sbuf(self.valT_scope, "valT", [128, KT, OWN], BF16, side="left")
        gw_dY = op("Y", lambda e: e.dma_start(
            out=self.gwTb[:], in_=self.gwT_d.rearrange("(a p) f -> p a f", p=128)), "dG", 16)

        with contextlib.ExitStack() as ph:
            selR0 = self.sbuf(ph, "selR0", [cfg.R, OCH * H * 128], BF16)
            selR1 = self.sbuf(ph, "selR1", [cfg.R, OCH * H * 128], BF16)
            csb0 = self.sbuf(ph, "csb0", [cfg.R, SC], BF16)
            csb1 = self.sbuf(ph, "csb1", [cfg.R, SC], BF16)
            cbh = [self.sbuf(ph, f"cbh{i}", [128, OWN], F32) for i in range(2)]
            cb_ps = [self.psum(ph, f"cb_ps{i}", [128, SC]) for i in range(2)]
            sel_dY = 0
            for dst, src in ((selR0, self.selR0_d), (selR1, self.selR1_d)):
                sel_dY = op("Y", lambda e, d=dst, sr=src: e.dma_start(out=d[:], in_=sr), "dG", 16)
            w("P", "st0", self.attn_dS[0])
            w("P", "st1", self.attn_dS[1])
            import os as _os
            if _os.environ.get("NO_CC"):
                op("P", lambda e: e.dma_start(out=self.cc_out.ap()[0:cfg.R, :],
                                              in_=self.cc_in.ap()), "cc", 16)
                op("P", lambda e: e.dma_start(out=self.cc_out.ap()[cfg.R:2 * cfg.R, :],
                                              in_=self.cc_in.ap()), "cc", 16)
                w("P", "cc", self.cnt["cc"])
            else:
                self.streams["P"].append(
                    lambda e: e.collective_compute(
                        "AllGather", ALU.bypass,
                        replica_groups=[[0, 1], [2, 3], [4, 5], [6, 7]],
                        ins=[self.cc_in.ap().opt()],
                        outs=[self.cc_out.ap().opt()]).then_inc(self.sems["cc"]))
                self.cnt["cc"] = 1
            w("P", "cc", 1)
            csd = op("P", lambda e: e.dma_start(out=csb0[:], in_=self.cc_out.ap()[0:cfg.R, :]),
                     "cs0", 16)
            csd = op("P", lambda e: e.dma_start(out=csb1[:], in_=self.cc_out.ap()[cfg.R:2 * cfg.R, :]),
                     "cs0", 16)
            bc_free = [0, 0]
            bi = 0
            val_V = 0
            for h in range(H):
                hpar = h % 2
                for i in range(OCH):
                    blk = i * H + h
                    par = bi % 2
                    w("T", "cs0", csd)
                    w("T", "dG", sel_dY)
                    w("T", "sV", bc_free[par])
                    op("T", lambda e, b=blk, p=par:
                       e.matmul(cb_ps[p][:], selR0[:, 128 * b:128 * (b + 1)], csb0[:],
                                start=True, stop=False), None)
                    tb = op("T", lambda e, b=blk, p=par:
                            e.matmul(cb_ps[p][:], selR1[:, 128 * b:128 * (b + 1)], csb1[:],
                                     start=False, stop=True), "sT")
                    w("V", "sT", tb)
                    w("V", "sV", val_V if i == 0 else 0)
                    bc_free[par] = op("V", lambda e, hp=hpar, i2=i, p=par:
                                      e.tensor_copy(cbh[hp][:, SC * i2:SC * (i2 + 1)], cb_ps[p][:]),
                                      "sV")
                    bi += 1
                w("V", "sV", bc_free[(bi - 1) % 2])
                val_V = op("V", lambda e, hh=h, hp=hpar:
                           e.tensor_mul(valT[:, hh, :], v_sb[:, hh, :], cbh[hp][:]), "sV")
            self.barrier()
        self.vsb_scope.close()
        if self.dump == "attn":
            self.dmp("valT", [KT * 128, OWN])
            self.dump_tiled("valT", valT)

        with contextlib.ExitStack() as ph:
            woh = [self.sbuf(ph, f"woh{i}", [128, KT, 128], BF16) for i in range(2)]
            xr = [self.sbuf(ph, f"xr{i}", [128, SC], F32) for i in range(2)]
            xn = [self.sbuf(ph, f"xn{i}", [128, SC], F32) for i in range(2)]
            xnsq = [self.sbuf(ph, f"xnsq{i}", [128, SC], F32) for i in range(2)]
            l0sb = self.sbuf(ph, "l0sb", [E, OWN], F32)
            self.l0sb = l0sb
            wo_ps = [self.psum(ph, f"wo_ps{i}", [128, SC]) for i in range(2)]
            l0_ps = [self.psum(ph, f"l0_ps{i}", [E, SC]) for i in range(OCH)]
            m2_ps = self.psum(ph, "m2_ps", [8, SC])
            q2_ps = self.psum(ph, "q2_ps", [8, SC])

            wo_dma = [0, 0]
            wo_freeT = [0, 0]
            xn_freeV = [0, 0]
            xn_lastT = [0, 0]
            xn_lastS = [0, 0]
            xn_lastY = [0, 0]
            i2 = 0
            total = KT * OCH
            for dt_i in range(KT):
                s2 = dt_i % 2
                w("P", "sT", wo_freeT[s2])
                wo_dma[s2] = op("P", lambda e, d2=dt_i, s3=s2:
                                e.dma_start(out=woh[s3][:],
                                            in_=self.wo_d.rearrange("(a p) f -> p a f", p=128)
                                            [:, :, 128 * d2:128 * (d2 + 1)]),
                                f"wo{s2}", 16)
                for c in range(OCH):
                    par = i2 % 2
                    w("T", f"wo{s2}", wo_dma[s2])
                    w("T", "dG", self.cnt["dG"])
                    w("T", "sV", xn_freeV[par])
                    for kt in range(KT):
                        tc = op("T", lambda e, k=kt, c2=c, p=par, s3=s2:
                                e.matmul(wo_ps[p][:], woh[s3][:, k, :],
                                         valT[:, k, SC * c2:SC * (c2 + 1)],
                                         start=(k == 0), stop=(k == KT - 1)),
                                "sT" if kt == KT - 1 else None)
                    if c == OCH - 1:
                        wo_freeT[s2] = self.cnt["sT"]
                    dyx = op("Y", lambda e, d2=dt_i, c2=c, p=par:
                             e.dma_start(out=xr[p][:],
                                         in_=self.xT[128 * d2:128 * (d2 + 1), SC * c2:SC * (c2 + 1)]),
                             f"xr{par}", 16)
                    w("V", "sT", tc)
                    w("V", f"xr{par}", dyx)
                    w("V", "sT", xn_lastT[par])
                    w("V", "sS", xn_lastS[par])
                    w("V", f"xo{par}", xn_lastY[par])
                    vv = op("V", lambda e, p=par: e.tensor_add(xn[p][:], xr[p][:], wo_ps[p][:]), "sV")
                    xn_freeV[par] = vv
                    w("Y", "sV", vv)
                    xn_lastY[par] = op("Y", lambda e, d2=dt_i, c2=c, p=par:
                                       e.dma_start(out=self.xnew_d[128 * d2:128 * (d2 + 1),
                                                                   SC * c2:SC * (c2 + 1)],
                                                   in_=xn[p][:]), f"xo{par}", 16)
                    w("S", "sV", vv)
                    ssq = op("S", lambda e, p=par:
                             e.activation(out=xnsq[p][:], in_=xn[p][:], func=AF.Square), "sS")
                    xn_lastS[par] = ssq
                    first = (i2 == 0)
                    last = (i2 == total - 1)
                    w("T", "sV", vv)
                    op("T", lambda e, c2=c, p=par, f=first, l=last:
                       e.matmul(m2_ps[:], self.oh8[:, 8 * c2:8 * (c2 + 1)], xn[p][:],
                                start=f, stop=l), None)
                    w("T", "sS", ssq)
                    op("T", lambda e, c2=c, p=par, f=first, l=last:
                       e.matmul(q2_ps[:], self.oh8[:, 8 * c2:8 * (c2 + 1)], xnsq[p][:],
                                start=f, stop=l), None)
                    tgl = op("T", lambda e, d2=dt_i, c2=c, p=par:
                             e.matmul(l0_ps[c2][:], self.gwTb[:, d2, :], xn[p][:],
                                      start=(d2 == 0), stop=(d2 == KT - 1)), "sT")
                    xn_lastT[par] = tgl
                    i2 += 1
            stats2_T = self.cnt["sT"]
            w("T", "sV", xn_freeV[0])
            w("T", "sV", xn_freeV[1])
            for kt in range(KT):
                tgs = op("T", lambda e, k=kt:
                         e.matmul(wo_ps[0][0:1, 0:E], self.ones128[:], self.gwTb[:, k, :],
                                  start=(k == 0), stop=(k == KT - 1)),
                         "sT" if kt == KT - 1 else None)
            w("V", "sT", tgs)
            gse = op("V", lambda e: e.tensor_copy(self.gsrow[:], wo_ps[0][0:1, 0:E]), "sV")
            w("T", "sV", gse)
            tb = op("T", lambda e: e.matmul(wo_ps[1][:, 0:E], self.sel8[0:1, 0:128], self.gsrow[:],
                                            start=True, stop=True), "sT")
            w("V", "sT", tb)
            gsb_V = op("V", lambda e: e.tensor_copy(self.gsum_b[:], wo_ps[1][:, 0:E]), "sV")
            w("V", "sT", stats2_T)
            op("V", lambda e: e.tensor_copy(self.nm2_row[:], m2_ps[:]), "sV")
            vv = op("V", lambda e: e.tensor_copy(self.r2w[:], q2_ps[:]), "sV")
            w("V", "sV", vv)
            vv = op("V", lambda e: e.tensor_mul(self.rstd2_row[:], self.nm2_row[:], self.nm2_row[:]), "sV")
            w("V", "sV", vv)
            vv = op("V", lambda e: e.tensor_sub(self.r2w[:], self.r2w[:], self.rstd2_row[:]), "sV")
            w("S", "sV", vv)
            ssq = op("S", lambda e: e.activation(out=self.r2w[:], in_=self.r2w[:],
                                                 func=AF.Sqrt, bias=self.epsb[0:8, :]), "sS")
            w("V", "sS", ssq)
            vv = op("V", lambda e: e.reciprocal(self.rstd2_row[:], self.r2w[:]), "sV")
            w("V", "sV", vv)
            nm2_V = op("V", lambda e: e.tensor_scalar_mul(self.nm2_row[:], self.nm2_row[:], -1.0), "sV")
            trf = gsb_V
            njj = min(OWN, SC) // 128
            for jj in range(njj):
                for src, dst in ((self.nm2_row, self.nm2T), (self.rstd2_row, self.rstd2T)):
                    w("T", "sV", max(nm2_V, trf))
                    tt2 = op("T", lambda e, j=jj, sr=src:
                             e.transpose(wo_ps[0][:, 0:8], sr[:, 128 * j:128 * (j + 1)],
                                         self.idf[0:8, 0:8]), "sT")
                    w("V", "sT", tt2)
                    trf = op("V", lambda e, j=jj, d=dst:
                             e.tensor_copy(d[:, j, :], wo_ps[0][:, 0:8]), "sV")
            l0ev = 0
            for c in range(OCH):
                w("V", "sT", stats2_T)
                l0ev = op("V", lambda e, c2=c:
                          e.tensor_copy(self.l0sb[:, SC * c2:SC * (c2 + 1)], l0_ps[c2][:]), "sV")
            trf2 = trf
            for tt in range(cfg.TT):
                w("T", "sV", max(l0ev, trf2))
                tt2 = op("T", lambda e, t=tt:
                         e.transpose(wo_ps[1][:, 0:E], self.l0sb[:, 128 * t:128 * (t + 1)],
                                     self.idf[0:E, 0:E]), "sT")
                w("V", "sT", tt2)
                trf2 = op("V", lambda e, t=tt:
                          e.tensor_copy(self.l0T[:, t, :], wo_ps[1][:, 0:E]), "sV")
            self.l0T_V = self.cnt["sV"]
            self.barrier()
        self.barrier()
        self.valT_scope.close()
        if self.dump == "attn":
            self.dmp("xnew", [cfg.D, OWN])
            op("Y", lambda e: e.dma_start(out=self.dumps["xnew"], in_=self.xnew_d), "dC", 16)

    def ph4_route(self):
        cfg = self.cfg
        E, CH, TT, OWN = cfg.E, cfg.CH, cfg.TT, cfg.OWN
        w, op = self.w, self.op
        with contextlib.ExitStack() as ph:
            lg = self.sbuf(ph, "lg", [128, E], F32)
            wk = self.sbuf(ph, "wk", [128, E], F32)
            msk = self.sbuf(ph, "msk", [128, E], F32)
            esb = self.sbuf(ph, "esb", [128, E], F32)
            cwt = self.sbuf(ph, "cwt", [128, TT, E], F32)
            cws = self.sbuf(ph, "cws", [128, TT, CH], F32)
            mx = self.sbuf(ph, "mx", [128, 6], F32)
            tr_ps = self.psum(ph, "tr_ps", [128, 128])

            prevS = 0
            vv = 0
            for tt in range(TT):
                c2, jj = (128 * tt) // SC, ((128 * tt) % SC) // 128
                w("V", "sV", self.l0T_V)
                w("V", "sS", prevS)
                w("V", "sV", vv)
                vv = op("V", lambda e, t=tt, c=c2, j=jj:
                        e.scalar_tensor_tensor(lg[:], self.gsum_b[:],
                                               self.nm2T[:, j, c:c + 1], self.l0T[:, t, :],
                                               ALU.mult, ALU.add), "sV")
                w("V", "sV", vv)
                vv = op("V", lambda e, c=c2, j=jj:
                        e.tensor_scalar(lg[:], lg[:], self.rstd2T[:, j, c:c + 1], None,
                                        ALU.mult), "sV")
                w("V", "sV", vv)
                vv = op("V", lambda e: e.reduce_max(mx[:, 0:1], lg[:], axis=AXX), "sV")
                w("V", "sV", vv)
                vv = op("V", lambda e: e.tensor_copy(wk[:], lg[:]), "sV")
                for it in range(3):
                    w("V", "sV", vv)
                    vv = op("V", lambda e, i=it: e.tensor_scalar(msk[:], wk[:], mx[:, i:i + 1],
                                                                 None, ALU.is_ge), "sV")
                    w("V", "sV", vv)
                    vv = op("V", lambda e: e.scalar_tensor_tensor(wk[:], msk[:], -1e30, wk[:],
                                                                  ALU.mult, ALU.add), "sV")
                    w("V", "sV", vv)
                    vv = op("V", lambda e, i=it: e.reduce_max(mx[:, i + 1:i + 2], wk[:], axis=AXX), "sV")
                w("V", "sV", vv)
                vv = op("V", lambda e: e.tensor_scalar(msk[:], lg[:], mx[:, 3:4], None, ALU.is_ge), "sV")
                w("V", "sV", vv)
                vv = op("V", lambda e: e.tensor_scalar_mul(mx[:, 4:5], mx[:, 0:1], -1.0), "sV")
                w("S", "sV", vv)
                prevS = op("S", lambda e: e.activation(out=esb[:], in_=lg[:], func=AF.Exp,
                                                       bias=mx[:, 4:5]), "sS")
                w("V", "sS", prevS)
                vv = op("V", lambda e: e.tensor_mul(esb[:], esb[:], msk[:]), "sV")
                w("V", "sV", vv)
                vv = op("V", lambda e: e.reduce_sum(mx[:, 5:6], esb[:], axis=AXX), "sV")
                w("V", "sV", vv)
                vv = op("V", lambda e: e.reciprocal(mx[:, 5:6], mx[:, 5:6]), "sV")
                w("V", "sV", vv)
                vv = op("V", lambda e, t=tt: e.tensor_scalar(cwt[:, t, :], esb[:], mx[:, 5:6],
                                                             None, ALU.mult), "sV")
                w("V", "sV", vv)
                vv = op("V", lambda e, t=tt: e.tensor_add(cws[:, t, :], cwt[:, t, 0:E:2],
                                                          cwt[:, t, 1:E:2]), "sV")
            route_V = self.cnt["sV"]
            trf = 0
            for src, dst, n in ((cwt, self.cwlT, E), (cws, self.cwsT, CH)):
                for tt in range(TT):
                    w("T", "sV", max(route_V, trf))
                    t1 = op("T", lambda e, t=tt, sr=src, nn=n:
                            e.transpose(tr_ps[0:nn, 0:128], sr[:, t, :], self.idf[:]), "sT")
                    w("V", "sT", t1)
                    trf = op("V", lambda e, t=tt, d=dst, nn=n:
                             e.tensor_copy(d[:, 128 * t:128 * (t + 1)], tr_ps[0:nn, 0:128]), "sV")
            self.barrier()
        self.barrier()
        if self.dump == "route":
            self.dmp("cwl", [E, OWN])
            self.dmp("cws", [CH, OWN])
            self.dump_flat("cwl", self.cwlT)
            self.dump_flat("cws", self.cwsT)

    def ph5_ffn(self):
        cfg = self.cfg
        KT, OWN, E, CH, IT, CIT, QL, NQ = (cfg.KT, cfg.OWN, cfg.E, cfg.CH,
                                           cfg.IT, cfg.CIT, cfg.QL, cfg.NQ)
        INTER, CI, D = cfg.INTER, cfg.CI, cfg.D
        w, op = self.w, self.op
        with contextlib.ExitStack() as ph:
            selE = self.sbuf(ph, "selE", [E, E * 128], BF16)
            selC = self.sbuf(ph, "selC", [CH, CH * 128], BF16)
            sel_dY = 0
            for dst, src in ((selE, self.selE_d), (selC, self.selC_d)):
                sel_dY = op("Y", lambda e, d=dst, sr=src: e.dma_start(out=d[:], in_=sr), "dE", 16)
            ln2q = self.sbuf(ph, "ln2q", [128, KT, QL], BF16)
            acc = self.sbuf(ph, "acc", [128, KT, QL], F32)
            xq = [self.sbuf(ph, f"xq{i}", [128, QL], F32) for i in range(2)]
            t2 = [self.sbuf(ph, f"t2{i}", [128, QL], F32) for i in range(2)]
            nmq = self.sbuf(ph, "nmq", [128, QL], F32)
            rsq = self.sbuf(ph, "rsq", [128, QL], F32)
            wgs = self.sbuf(ph, "wgs", [128, KT, INTER], BF16)
            wus = self.sbuf(ph, "wus", [128, KT, INTER], BF16)
            wds = self.sbuf(ph, "wds", [128, IT, D], BF16)
            cgs = self.sbuf(ph, "cgs", [128, KT, CI], BF16)
            cus = self.sbuf(ph, "cus", [128, KT, CI], BF16)
            cds = self.sbuf(ph, "cds", [128, CIT, D], BF16)
            sgs = [self.sbuf(ph, "sgs0", [128, IT, QL], BF16)] * 2
            ums = [self.sbuf(ph, "ums0", [128, IT, QL], BF16)] * 2
            asb = [self.sbuf(ph, f"asb{i}", [128, IT, QL], BF16) for i in range(2)]
            cwb = [self.sbuf(ph, f"cwb{i}", [128, QL], BF16) for i in range(2)]
            gu_ps = [self.psum(ph, f"gu_ps{i}", [128, QL]) for i in range(4)]
            d_ps = [self.psum(ph, f"d_ps{i}", [128, QL]) for i in range(2)]
            cw_ps = [self.psum(ph, f"cw_ps{i}", [128, QL]) for i in range(2)]

            state = {
                "slot_freeT": {},
                "gu_freeS": [0] * 4, "gu_freeV": [0] * 4,
                "d_freeV": [0, 0], "cw_freeV": [0, 0],
                "sg_freeP": [0, 0], "um_freeP": [0, 0], "a_freeT": [0, 0],
                "xq_freeV": [0, 0], "acc_dY": 0, "ln2q_P": 0, "gu_lastT": 0,
                "cwb_V": [0, 0],
            }

            for qp in range(NQ):
                q0 = qp * QL
                nbc = QL // 128
                bfree = state["cw_freeV"][0]
                for j in range(nbc):
                    t0 = q0 + 128 * j
                    c2, jj = t0 // SC, (t0 % SC) // 128
                    for src, dst in ((self.nm2_row, nmq), (self.rstd2_row, rsq)):
                        w("T", "sV", bfree)
                        tb = op("T", lambda e, c=c2, j2=jj, sr=src:
                                e.matmul(cw_ps[0][:, 0:128],
                                         self.sel8[:, 128 * c:128 * (c + 1)],
                                         sr[:, 128 * j2:128 * (j2 + 1)],
                                         start=True, stop=True), "sT")
                        w("V", "sT", tb)
                        if j == 0 and src is self.nm2_row and qp > 0:
                            w("V", "sP", state["ln2q_P"])
                        bfree = op("V", lambda e, d=dst, j2=j:
                                   e.tensor_copy(d[:, 128 * j2:128 * (j2 + 1)],
                                                 cw_ps[0][:, 0:128]), "sV")
                state["cw_freeV"][0] = bfree
                bc_V = self.cnt["sV"]
                lastP = 0
                t2_freeP = state.setdefault("t2_freeP", [0, 0])
                for kt in range(KT):
                    par = kt % 2
                    w("Y", "sV", state["xq_freeV"][par])
                    dy = op("Y", lambda e, k=kt, p=par, q=q0:
                            e.dma_start(out=xq[p][:],
                                        in_=self.xnew_d[128 * k:128 * (k + 1), q:q + QL]),
                            f"xq{par}", 16)
                    w("V", f"xq{par}", dy)
                    if kt < 2:
                        w("V", "sV", bc_V)
                        w("V", "sT", state["gu_lastT"])
                    w("V", "sP", t2_freeP[par])
                    vv = op("V", lambda e, p=par: e.tensor_mul(t2[p][:], xq[p][:], rsq[:]), "sV")
                    state["xq_freeV"][par] = vv
                    w("P", "sV", vv)
                    lastP = op("P", lambda e, k=kt, p=par:
                               e.tensor_add(ln2q[:, k, :], t2[p][:], nmq[:]), "sP")
                    t2_freeP[par] = lastP
                state["ln2q_P"] = lastP
                w("Y", "ao", state["acc_dY"])
                state["seed_dY"] = op("Y", lambda e, q=q0: e.dma_start(
                    out=acc[:],
                    in_=self.xnew_d.rearrange("(a p) f -> p a f", p=128)[:, :, q:q + QL]),
                    "ao", 16)

                def expert(idx, wgd, wud, wdd, wgt, wut, wdt, sg, um, aa, cwT,
                           selT, nIT, kind, scale):
                    st = state
                    apar = idx % 2
                    for nm, wt2, wdram in ((kind + "g", wgt, wgd), (kind + "u", wut, wud),
                                           (kind + "d", wdt, wdd)):
                        w("P", "sT", st["slot_freeT"].get(nm, 0))
                        st["slot_freeT"][nm + "_dma"] = op(
                            "P", lambda e, i=idx, t3=wt2, dr=wdram:
                            e.dma_start(out=t3[:], in_=dr[i].rearrange("(a p) f -> p a f", p=128)),
                            nm, 16)
                    dg = st["slot_freeT"][kind + "g_dma"]
                    du = st["slot_freeT"][kind + "u_dma"]
                    dd = st["slot_freeT"][kind + "d_dma"]
                    w("T", "sV", st["cw_freeV"][apar])
                    w("T", "dE", sel_dY)
                    tb = op("T", lambda e, i=idx, p=apar, ct=cwT, sl=selT, q=q0:
                            e.matmul(cw_ps[p][:], sl[:, 128 * i:128 * (i + 1)],
                                     ct[:, q:q + QL], start=True, stop=True), "sT")
                    w("V", "sT", tb)
                    w("V", "sV", st["cwb_V"][apar])
                    cwv = op("V", lambda e, p=apar, sc2=scale:
                             e.tensor_scalar_mul(cwb[p][:], cw_ps[p][:], sc2), "sV")
                    st["cw_freeV"][apar] = cwv
                    for gu, wt2, dmac in ((0, wgt, dg), (1, wut, du)):
                        for oi in range(nIT):
                            pp = 2 * gu + (oi % 2)
                            w("T", kind + ("g" if gu == 0 else "u"), dmac)
                            if gu == 0:
                                w("T", "sS", st["gu_freeS"][pp])
                            else:
                                w("T", "sV", st["gu_freeV"][pp])
                            w("T", "sP", st["ln2q_P"])
                            for kt in range(KT):
                                tc = op("T", lambda e, k=kt, o=oi, w2=wt2, p=pp:
                                        e.matmul(gu_ps[p][:], w2[:, k, 128 * o:128 * (o + 1)],
                                                 ln2q[:, k, :], start=(k == 0), stop=(k == KT - 1)),
                                        "sT" if kt == KT - 1 else None)
                            if gu == 0 and oi == nIT - 1:
                                st["slot_freeT"][kind + "g"] = tc
                            if gu == 1 and oi == nIT - 1:
                                st["slot_freeT"][kind + "u"] = tc
                            if gu == 0:
                                w("S", "sT", tc)
                                w("S", "sP", st["sg_freeP"][apar])
                                st["gu_freeS"][pp] = op(
                                    "S", lambda e, o=oi, p=pp, a2=apar, sg2=sg:
                                    e.activation(out=sg2[a2][:, o, :], in_=gu_ps[p][:],
                                                 func=AF.Silu), "sS")
                            else:
                                w("V", "sT", tc)
                                w("V", "sP", st["um_freeP"][apar])
                                w("V", "sV", cwv)
                                st["gu_freeV"][pp] = op(
                                    "V", lambda e, o=oi, p=pp, a2=apar, um2=um:
                                    e.tensor_mul(um2[a2][:, o, :], gu_ps[p][:], cwb[a2][:]), "sV")
                    st["cwb_V"][apar] = self.cnt["sV"]
                    silu_S = self.cnt["sS"]
                    um_V = self.cnt["sV"]
                    w("P", "sS", silu_S)
                    w("P", "sV", um_V)
                    w("P", "sT", st["a_freeT"][apar])
                    pa = 0
                    for oi in range(nIT):
                        pa = op("P", lambda e, o=oi, a2=apar, sg2=sg, um2=um, aa2=aa:
                                e.tensor_mul(aa2[a2][:, o, :], sg2[a2][:, o, :], um2[a2][:, o, :]),
                                "sP")
                    st["sg_freeP"][0] = st["sg_freeP"][1] = pa
                    st["um_freeP"][0] = st["um_freeP"][1] = pa
                    w("T", "sP", pa)
                    w("T", kind + "d", dd)
                    for oi in range(KT):
                        dp = oi % 2
                        w("T", "sV", st["d_freeV"][dp])
                        for it2 in range(nIT):
                            tc = op("T", lambda e, o=oi, i3=it2, p=dp, a2=apar, w3=wdt, aa2=aa:
                                    e.matmul(d_ps[p][:], w3[:, i3, 128 * o:128 * (o + 1)],
                                             aa2[a2][:, i3, :], start=(i3 == 0), stop=(i3 == nIT - 1)),
                                    "sT" if it2 == nIT - 1 else None)
                        if oi == KT - 1:
                            st["a_freeT"][apar] = tc
                            st["slot_freeT"][kind + "d"] = tc
                        w("V", "sT", tc)
                        if oi < 2 and idx == 0 and kind == "L":
                            w("V", "ao", state["seed_dY"])
                        st["d_freeV"][dp] = op("V", lambda e, o=oi, p=dp:
                                               e.tensor_add(acc[:, o, :], acc[:, o, :], d_ps[p][:]),
                                               "sV")
                    state["gu_lastT"] = max(state["gu_lastT"],
                                            st["slot_freeT"][kind + "g"],
                                            st["slot_freeT"][kind + "u"])

                for e_i in range(E):
                    expert(e_i, self.wg_d, self.wu_d, self.wd_d, wgs, wus, wds,
                           sgs, ums, asb, self.cwlT, selE, IT, "L", 1.0)
                for e_i in range(CH):
                    expert(e_i, self.cwg_d, self.cwu_d, self.cwd_d, cgs, cus, cds,
                           sgs, ums, asb, self.cwsT, selC, CIT, "S", 0.05)
                ffn_V = self.cnt["sV"]
                w("Y", "sV", ffn_V)
                state["acc_dY"] = op("Y", lambda e, q=q0: e.dma_start(
                    out=self.out_d.rearrange("(a p) f -> p a f", p=128)[:, :, q:q + QL],
                    in_=acc[:]), "ao", 16)
            self.barrier()

    def finish(self):
        for nm in ("valT_scope", "vsb_scope", "ln1_oth_scope", "ln1_own_scope"):
            scp = getattr(self, nm, None)
            if scp is not None:
                scp.close()
                setattr(self, nm, None)
        for s, v in self.cnt.items():
            self.w("Y", s, v)
        streams = self.streams
        with self.nc.Block() as block:
            block.tensor(lambda e: [f(e) for f in streams["T"]])
            block.scalar(lambda e: [f(e) for f in streams["S"]])
            block.vector(lambda e: [f(e) for f in streams["V"]])
            block.gpsimd(lambda e: [f(e) for f in streams["P"]])
            block.sync(lambda e: [f(e) for f in streams["Y"]])
        self.es.close()
        return self.nc


def build_program(cfg, dump=None):
    b = Builder(cfg, dump)
    b.declare_io()
    b.load_consts()
    b.ph1_ln1()
    if dump == "ln1":
        return b.finish()
    b.ph2_attn()
    b.ph2b_vproj()
    b.ph3_val_wo()
    if dump == "attn":
        return b.finish()
    b.ph4_route()
    if dump == "route":
        return b.finish()
    b.ph5_ffn()
    return b.finish()


def layout_inputs(cfg, inputs, n_cores=8):
    x = np.asarray(inputs["L0_attn_x"])
    wqkv = np.asarray(inputs["L0_attn_w_qkv"])
    wo = np.asarray(inputs["L0_attn_w_o"])
    gate = np.asarray(inputs["L0_ffn_gate_w"])
    consts = mk_consts(cfg)
    H, KT, OWN = cfg.H, cfg.KT, cfg.OWN
    wq2 = wqkv.reshape(KT, 128, 3, H, cfg.DH)
    wqkh = np.ascontiguousarray(
        np.transpose(wq2, (3, 0, 1, 2, 4)).reshape(H, KT, 128, 3 * cfg.DH))
    gwT = np.ascontiguousarray(gate.T)
    selRs = [mk_selR(cfg, hf) for hf in (0, 1)]
    in_maps = []
    for c in range(n_cores):
        b_i, half = c // 2, c % 2
        xTb = x[b_i].T
        own = xTb[:, half * OWN:(half + 1) * OWN]
        oth = xTb[:, (1 - half) * OWN:(2 - half) * OWN]
        m = {
            "xT": np.ascontiguousarray(np.concatenate([own, oth], axis=1)),
            "wqkh": wqkh,
            "wo": wo,
            "gwT": gwT,
            "wg": np.asarray(inputs["L0_ffn_Wg"]),
            "wu": np.asarray(inputs["L0_ffn_Wu"]),
            "wd": np.asarray(inputs["L0_ffn_Wd"]),
            "cwg": np.asarray(inputs["L0_ffn_cWg"]),
            "cwu": np.asarray(inputs["L0_ffn_cWu"]),
            "cwd": np.asarray(inputs["L0_ffn_cWd"]),
            "selR0": selRs[half][0],
            "selR1": selRs[half][1],
        }
        for k in ("idf", "oh8", "sel8", "ones128", "epsb", "selE", "selC"):
            m[k] = consts[k]
        in_maps.append(m)
    return in_maps


_CACHE = {}


def kernel(**inputs):
    cfg = FULL
    in_maps = layout_inputs(cfg, inputs)
    if "nc" not in _CACHE:
        _CACHE["nc"] = build_program(cfg)
    res = run_bass_kernel_spmd(_CACHE["nc"], in_maps, core_ids=list(range(8)))
    B, S, D, OWN = 4, 4096, cfg.D, cfg.OWN
    out = np.empty((B, S, D), dtype=np.float32)
    for c in range(8):
        b_i, half = c // 2, c % 2
        out[b_i, half * OWN:(half + 1) * OWN, :] = res.results[c]["out"].T
    return out


# version 03
# speedup vs baseline: 1.1532x; 1.1532x over previous
import contextlib
import math
import numpy as np
import ml_dtypes
import concourse.bass as bass
import concourse.mybir as mybir
from concourse.bass_utils import run_bass_kernel_spmd

F32 = mybir.dt.float32
BF16 = mybir.dt.bfloat16
AF = mybir.ActivationFunctionType
ALU = mybir.AluOpType
AXX = mybir.AxisListType.X

SC = 512
EPS = 1e-5


class CFG:
    def __init__(self, D=2048, TOK=4096, OWN=2048, H=16, E=32, INTER=768,
                 CH=16, CI=128, NQ=4):
        self.D, self.TOK, self.OWN, self.H = D, TOK, OWN, H
        self.DH = 128
        self.E, self.INTER, self.CH, self.CI, self.NQ = E, INTER, CH, CI, NQ
        self.KT = D // 128
        self.QT = OWN // 128
        self.TT = OWN // 128
        self.KCH = TOK // SC
        self.OCH = OWN // SC
        self.EXPC = min(1024, OWN)
        self.NEX = TOK // self.EXPC
        self.IT = INTER // 128
        self.CIT = CI // 128
        self.QL = OWN // NQ
        self.R = self.KCH * H
        assert TOK == 2 * OWN and self.QL <= SC and OWN % SC == 0
        assert self.R <= 128 and D == H * self.DH and self.QL % 128 == 0


FULL = CFG()


def mk_consts(cfg):
    c = {}
    c["idf"] = np.eye(128, dtype=np.float32)
    oh = np.zeros((128, 8 * 8), dtype=np.float32)
    for ch in range(8):
        oh[:, 8 * ch + ch] = 1.0 / cfg.D
    c["oh8"] = oh
    s8 = np.zeros((8, 8 * 128), dtype=np.float32)
    for r in range(8):
        s8[r, 128 * r:128 * (r + 1)] = 1.0
    c["sel8"] = s8
    c["ones128"] = np.ones((128, 1), dtype=np.float32)
    c["epsb"] = np.full((128, 1), 1e-5, dtype=np.float32)
    sE = np.zeros((cfg.E, cfg.E * 128), dtype=ml_dtypes.bfloat16)
    for e in range(cfg.E):
        sE[e, 128 * e:128 * (e + 1)] = 1.0
    c["selE"] = sE
    sC = np.zeros((cfg.CH, cfg.CH * 128), dtype=ml_dtypes.bfloat16)
    for e in range(cfg.CH):
        sC[e, 128 * e:128 * (e + 1)] = 1.0
    c["selC"] = sC
    return c


def mk_selR(cfg, half):
    R, H, OCH = cfg.R, cfg.H, cfg.OCH
    s0 = np.zeros((R, OCH * H * 128), dtype=ml_dtypes.bfloat16)
    s1 = np.zeros((R, OCH * H * 128), dtype=ml_dtypes.bfloat16)
    for i in range(OCH):
        for h in range(H):
            blk = i * H + h
            mine = s0 if half == 0 else s1
            peer = s1 if half == 0 else s0
            mine[i * H + h, 128 * blk:128 * (blk + 1)] = 1.0
            peer[(OCH + i) * H + h, 128 * blk:128 * (blk + 1)] = 1.0
    return s0, s1


class Builder:
    def __init__(self, cfg, dump=None):
        self.cfg = cfg
        self.dump = dump
        self.nc = bass.Bass("TRN2", target_bir_lowering=False, debug=False)
        self.streams = {k: [] for k in "TSVPY"}
        self.cnt = {}
        self.sems = {}
        self.es = contextlib.ExitStack()
        for s in ("dC", "dD", "dG", "dE", "x0", "x1", "wq0", "wq1", "st0", "st1", "wv0", "wv1",
                  "cs0", "wo0", "wo1", "xr0", "xr1", "xo0", "xo1",
                  "xq0", "xq1", "xq2", "xq3", "ao", "Lg", "Lu", "Ld", "Sg", "Su", "Sd",
                  "Mg", "Mu", "Md", "Bg", "Bu", "Bd", "se0", "se1",
                  "sT", "sS", "sV", "sP", "cc"):
            self.sems[s] = self.es.enter_context(self.nc.semaphore(s))
            self.cnt[s] = 0
        self.dumps = {}

    def w(self, eng, s, val):
        if val > 0:
            self.streams[eng].append(lambda e, ss=self.sems[s], v=val: e.wait_ge(ss, v))

    def op(self, eng, fn, inc=None, n=1):
        if inc is None:
            self.streams[eng].append(fn)
            return None
        self.cnt[inc] += n
        self.streams[eng].append(
            lambda e, fn=fn, ss=self.sems[inc], nn=n: fn(e).then_inc(ss, nn))
        return self.cnt[inc]

    def barrier(self):
        snap = dict(self.cnt)
        for eng in "TSVPY":
            for s, v in snap.items():
                self.w(eng, s, v)

    def sbuf(self, scp, name, shape, dt, side="right"):
        return scp.enter_context(self.nc.sbuf_tensor("sb_" + name, shape, dt, side=side))

    def psum(self, scp, name, shape, dt=F32):
        return scp.enter_context(self.nc.psum_tensor("ps_" + name, shape, dt))

    def dmp(self, name, shape, dt=F32):
        self.dumps[name] = self.nc.dram_tensor("dbg_" + name, shape, dt,
                                               kind="ExternalOutput").ap()
        return self.dumps[name]

    def dump_tiled(self, name, t):
        d = self.dumps[name]
        self.op("P", lambda e, d=d, t=t: e.dma_start(
            out=d.rearrange("(a p) f -> p a f", p=128), in_=t[:]), "dD", 16)

    def dump_flat(self, name, t):
        d = self.dumps[name]
        self.op("P", lambda e, d=d, t=t: e.dma_start(out=d, in_=t[:]), "dD", 16)

    def declare_io(self):
        nc, cfg = self.nc, self.cfg
        D, TOK, H, E, CH, CI, KT, OWN = (cfg.D, cfg.TOK, cfg.H, cfg.E, cfg.CH,
                                         cfg.CI, cfg.KT, cfg.OWN)
        ap = lambda *a, **k: nc.dram_tensor(*a, **k).ap()
        self.xT = ap("xT", [D, TOK], F32, kind="ExternalInput")
        self.wqkh = ap("wqkh", [H, KT, 128, 384], F32, kind="ExternalInput")
        self.wo_d = ap("wo", [D, D], F32, kind="ExternalInput")
        self.gwT_d = ap("gwT", [D, E], F32, kind="ExternalInput")
        self.wg_d = ap("wg", [E, D, cfg.INTER], F32, kind="ExternalInput")
        self.wu_d = ap("wu", [E, D, cfg.INTER], F32, kind="ExternalInput")
        self.wd_d = ap("wd", [E, cfg.INTER, D], F32, kind="ExternalInput")
        self.cwg_d = ap("cwg", [CH, D, CI], F32, kind="ExternalInput")
        self.cwu_d = ap("cwu", [CH, D, CI], F32, kind="ExternalInput")
        self.cwd_d = ap("cwd", [CH, CI, D], F32, kind="ExternalInput")
        self.idf_d = ap("idf", [128, 128], F32, kind="ExternalInput")
        self.oh8_d = ap("oh8", [128, 64], F32, kind="ExternalInput")
        self.sel8_d = ap("sel8", [8, 1024], F32, kind="ExternalInput")
        self.ones_d = ap("ones128", [128, 1], F32, kind="ExternalInput")
        self.epsb_d = ap("epsb", [128, 1], F32, kind="ExternalInput")
        self.selR0_d = ap("selR0", [cfg.R, cfg.OCH * H * 128], BF16, kind="ExternalInput")
        self.selR1_d = ap("selR1", [cfg.R, cfg.OCH * H * 128], BF16, kind="ExternalInput")
        self.selE_d = ap("selE", [E, E * 128], BF16, kind="ExternalInput")
        self.selC_d = ap("selC", [CH, CH * 128], BF16, kind="ExternalInput")
        self.out_d = ap("out", [D, OWN], F32, kind="ExternalOutput")
        self.xnew_d = ap("xnew", [D, OWN], F32)
        self.cc_in = nc.dram_tensor("cc_in", [cfg.R, SC], F32)
        self.cc_out = nc.dram_tensor("cc_out", [2 * cfg.R, SC], F32)

    def load_consts(self):
        s = self.es
        self.idf = self.sbuf(s, "idf", [128, 128], F32, side="left")
        self.oh8 = self.sbuf(s, "oh8", [128, 64], F32, side="left")
        self.sel8 = self.sbuf(s, "sel8", [8, 1024], F32, side="left")
        self.ones128 = self.sbuf(s, "ones", [128, 1], F32, side="left")
        self.epsb = self.sbuf(s, "epsb", [128, 1], F32, side="left")
        cfg = self.cfg
        self.nm2_row = self.sbuf(s, "nm2_row", [8, SC], F32, side="left")
        self.rstd2_row = self.sbuf(s, "rstd2_row", [8, SC], F32, side="left")
        self.r2w = self.sbuf(s, "r2w", [8, SC], F32, side="left")
        self.nm2T = self.sbuf(s, "nm2T", [128, 4, 8], F32, side="left")
        self.rstd2T = self.sbuf(s, "rstd2T", [128, 4, 8], F32, side="left")
        self.l0T = self.sbuf(s, "l0T", [128, cfg.TT, cfg.E], F32, side="left")
        self.gsum_b = self.sbuf(s, "gsum_b", [128, cfg.E], F32, side="left")
        self.gwTb = self.sbuf(s, "gwTb", [128, cfg.KT, cfg.E], F32, side="left")
        self.gsrow = self.sbuf(s, "gsrow", [1, cfg.E], F32, side="left")
        self.cwlT = self.sbuf(s, "cwlT", [cfg.E, cfg.OWN], BF16, side="left")
        self.cwsT = self.sbuf(s, "cwsT", [cfg.CH, cfg.OWN], BF16, side="left")
        for dst, src in ((self.idf, self.idf_d), (self.oh8, self.oh8_d),
                         (self.sel8, self.sel8_d), (self.ones128, self.ones_d),
                         (self.epsb, self.epsb_d)):
            self.op("Y", lambda e, d=dst, sr=src: e.dma_start(out=d[:], in_=sr), "dC", 16)
        v = self.cnt["dC"]
        for eng in "TSVP":
            self.w(eng, "dC", v)

    def ph1_ln1(self):
        cfg = self.cfg
        KT, TOK, OWN, EXPC, KCH = cfg.KT, cfg.TOK, cfg.OWN, cfg.EXPC, cfg.KCH
        w, op = self.w, self.op
        self.ln1_own_scope = contextlib.ExitStack()
        self.ln1_oth_scope = contextlib.ExitStack()
        self.ln1_own = self.sbuf(self.ln1_own_scope, "ln1_own", [128, KT, OWN], BF16, side="left")
        self.ln1_oth = self.sbuf(self.ln1_oth_scope, "ln1_oth", [128, KT, OWN], BF16, side="left")
        with contextlib.ExitStack() as ph:
            xs = [self.sbuf(ph, f"xs{i}", [128, EXPC], F32) for i in range(2)]
            xsq = [self.sbuf(ph, f"xsq{i}", [128, EXPC], F32) for i in range(2)]
            rows = self.sbuf(ph, "rows", [8, SC * 4], F32)
            nmrb = [self.sbuf(ph, f"nmrb{i}", [128, EXPC], F32) for i in range(2)]
            rstdb = [self.sbuf(ph, f"rstdb{i}", [128, EXPC], F32) for i in range(2)]
            mean_ps = self.psum(ph, "mean_ps", [8, SC])
            msq_ps = self.psum(ph, "msq_ps", [8, SC])
            bc_ps = [self.psum(ph, f"bc_ps{i}", [128, SC]) for i in range(2)]

            nch = TOK // EXPC
            total = KT * nch
            xs_free = {}
            i = 0
            for kt in range(KT):
                for chh in range(nch):
                    par = i % 2
                    if i >= 2:
                        w("Y", "sT", xs_free[i - 2])
                    dy = op("Y", lambda e, p=par, k=kt, c=chh:
                            e.dma_start(out=xs[p][:],
                                        in_=self.xT[128 * k:128 * (k + 1), EXPC * c:EXPC * (c + 1)]),
                            f"x{par}", 16)
                    w("S", f"x{par}", dy)
                    ss = op("S", lambda e, p=par:
                            e.activation(out=xsq[p][:], in_=xs[p][:], func=AF.Square), "sS")
                    w("T", f"x{par}", dy)
                    w("T", "sS", ss)
                    for j in range(EXPC // SC):
                        c512 = chh * (EXPC // SC) + j
                        first = (i == 0 and j == 0)
                        last = (i == total - 1 and j == (EXPC // SC) - 1)
                        op("T", lambda e, p=par, c=c512, f=first, l=last, jj=j:
                           e.matmul(mean_ps[:], self.oh8[:, 8 * c:8 * (c + 1)],
                                    xs[p][:, SC * jj:SC * (jj + 1)], start=f, stop=l), None)
                        tl = op("T", lambda e, p=par, c=c512, f=first, l=last, jj=j:
                                e.matmul(msq_ps[:], self.oh8[:, 8 * c:8 * (c + 1)],
                                         xsq[p][:, SC * jj:SC * (jj + 1)], start=f, stop=l), "sT")
                    xs_free[i] = tl
                    i += 1
            stats_T = self.cnt["sT"]
            sS_total = self.cnt["sS"] + 1
            w("V", "sT", stats_T)
            op("V", lambda e: e.tensor_copy(rows[:, 0:SC], mean_ps[:]), "sV")
            vv = op("V", lambda e: e.tensor_copy(rows[:, SC:2 * SC], msq_ps[:]), "sV")
            w("V", "sV", vv)
            vv = op("V", lambda e: e.tensor_mul(rows[:, 2 * SC:3 * SC], rows[:, 0:SC], rows[:, 0:SC]), "sV")
            w("V", "sV", vv)
            vv = op("V", lambda e: e.tensor_sub(rows[:, SC:2 * SC], rows[:, SC:2 * SC],
                                                rows[:, 2 * SC:3 * SC]), "sV")
            w("S", "sV", vv)
            ssq = op("S", lambda e: e.activation(out=rows[:, 2 * SC:3 * SC], in_=rows[:, SC:2 * SC],
                                                 func=AF.Sqrt, bias=self.epsb[0:8, :]), "sS")
            w("V", "sS", ssq)
            vv = op("V", lambda e: e.reciprocal(rows[:, 2 * SC:3 * SC], rows[:, 2 * SC:3 * SC]), "sV")
            w("V", "sV", vv)
            op("V", lambda e: e.scalar_tensor_tensor(rows[:, 3 * SC:4 * SC], rows[:, 0:SC], -1.0,
                                                     rows[:, 2 * SC:3 * SC], ALU.mult, ALU.mult), "sV")
            rows_V = self.cnt["sV"]
            i = 0
            ap_free = {}
            xs_freeV = {}
            ap_P = {}
            for chh in range(nch):
                cpar = chh % 2
                bc_free = 0
                for j in range(EXPC // SC):
                    c512 = chh * (EXPC // SC) + j
                    for roff, dstl in ((3 * SC, nmrb), (2 * SC, rstdb)):
                        par = (2 * j + (0 if roff == 3 * SC else 1)) % 2
                        w("T", "sV", rows_V)
                        w("T", "sV", bc_free)
                        tb = op("T", lambda e, cc=c512, ro=roff, p=par:
                                e.matmul(bc_ps[p][:], self.sel8[:, 128 * cc:128 * (cc + 1)],
                                         rows[:, ro:ro + SC], start=True, stop=True), "sT")
                        w("V", "sT", tb)
                        w("V", "sP", ap_P.get(chh - 2, 0))
                        bc_free = op("V", lambda e, d=dstl[cpar], jj=j, p=par:
                                     e.tensor_copy(d[:, SC * jj:SC * (jj + 1)], bc_ps[p][:]), "sV")
                bc_done = self.cnt["sV"]
                g0 = chh * EXPC
                if g0 < OWN:
                    tdst, loc = self.ln1_own, g0
                else:
                    tdst, loc = self.ln1_oth, g0 - OWN
                for kt in range(KT):
                    par = i % 2
                    w("Y", "sP", ap_free.get(i - 2, 0))
                    w("Y", "sV", xs_freeV.get(i - 2, 0))
                    if i < 2:
                        w("Y", "sT", stats_T)
                    dy = op("Y", lambda e, p=par, k=kt, c=chh:
                            e.dma_start(out=xs[p][:],
                                        in_=self.xT[128 * k:128 * (k + 1), EXPC * c:EXPC * (c + 1)]),
                            f"x{par}", 16)
                    w("V", f"x{par}", dy)
                    if kt == 0:
                        w("V", "sV", bc_done)
                    if i >= 2:
                        w("V", "sP", ap_free[i - 2])
                    w("V", "sS", sS_total)
                    vv = op("V", lambda e, p=par, cp=cpar:
                            e.tensor_mul(xsq[p][:], xs[p][:], rstdb[cp][:]), "sV")
                    xs_freeV[i] = vv
                    w("P", "sV", vv)
                    ap_free[i] = op("P", lambda e, p=par, k=kt, td=tdst, lc=loc, cp=cpar:
                                    e.tensor_add(td[:, k, lc:lc + EXPC], xsq[p][:],
                                                 nmrb[cp][:]), "sP")
                    i += 1
                ap_P[chh] = self.cnt["sP"]
            self.barrier()
        self.barrier()
        if self.dump == "ln1":
            self.dmp("ln1o", [self.cfg.KT * 128, OWN])
            self.dmp("ln1x", [self.cfg.KT * 128, OWN])
            self.dump_tiled("ln1o", self.ln1_own)
            self.dump_tiled("ln1x", self.ln1_oth)

    def ph2_attn(self):
        cfg = self.cfg
        KT, OWN, TOK, H, QT = cfg.KT, cfg.OWN, cfg.TOK, cfg.H, cfg.QT
        KCH, OCH, NEX, EXPC = cfg.KCH, cfg.OCH, cfg.NEX, cfg.EXPC
        DHSC = 1.0 / math.sqrt(cfg.DH)
        w, op = self.w, self.op

        def ln1_src(c):
            if c < OCH:
                return self.ln1_own, c * SC
            return self.ln1_oth, (c - OCH) * SC

        with contextlib.ExitStack() as ph:
            wqk = [self.sbuf(ph, f"wqk{i}", [128, KT, 256], BF16) for i in range(2)]
            qh = self.sbuf(ph, "qh", [128, OWN], BF16)
            kh = self.sbuf(ph, "kh", [128, TOK], BF16)
            exp_sb = [self.sbuf(ph, f"exp{i}", [128, TOK], BF16) for i in range(2)]
            zw = self.sbuf(ph, "zw", [128, 2 * (NEX + 2)], F32)
            rzb = [self.sbuf(ph, f"rzb{i}", [128, 1], BF16) for i in range(2)]
            stg = [self.sbuf(ph, f"stg{i}", [1, SC], F32) for i in range(2)]
            ncolb = (KCH + 2) // 3
            sc_ps = [self.psum(ph, f"sc_ps{i}", [128, EXPC]) for i in range(2)]
            qk_ps = self.psum(ph, "qk_ps", [128, SC])
            col_ps = [self.psum(ph, f"col_ps{i}", [128, SC]) for i in range(ncolb)]

            wq_dma = [0, 0]
            wq_freeT = [0, 0]
            qk_evac = 0
            scps_free = [0, 0]
            ebuf_free = [0, 0]
            col_evacV = 0
            stg_dma = [0, 0]
            prev_scoresT = 0

            for h in range(H):
                slot = h % 2
                w("P", "sT", wq_freeT[slot])
                wq_dma[slot] = op("P", lambda e, hh=h, s=slot:
                                  e.dma_start(out=wqk[s][:],
                                              in_=self.wqkh[hh].rearrange("a p f -> p a f")[:, :, 0:256]),
                                  f"wq{slot}", 16)
                qkv_specs = (
                    [(0, self.ln1_own, c * SC, "q", c * SC) for c in range(OCH)] +
                    [(128,) + ln1_src(c) + ("k", c * SC) for c in range(KCH)]
                )
                firstspec = True
                qkv_evs = [qk_evac, col_evacV]
                for si, (woff, srcT, srcloc, which, dstloc) in enumerate(qkv_specs):
                    qpsb = (qk_ps, col_ps[0])[si % 2]
                    w("T", f"wq{slot}", wq_dma[slot])
                    w("T", "sV", qkv_evs[si % 2])
                    for kt in range(KT):
                        tc = op("T", lambda e, s=slot, k=kt, wf=woff, st=srcT, sl=srcloc, qp2=qpsb:
                                e.matmul(qp2[0:128, 0:SC], wqk[s][:, k, wf:wf + 128],
                                         st[:, k, sl:sl + SC],
                                         start=(k == 0), stop=(k == KT - 1)),
                                "sT" if kt == KT - 1 else None)
                    w("V", "sT", tc)
                    if firstspec:
                        w("V", "sT", prev_scoresT)
                        firstspec = False
                    dtgt = qh if which == "q" else kh
                    qkv_evs[si % 2] = op("V", lambda e, dl=dstloc, d=dtgt, qp2=qpsb:
                                         e.tensor_copy(d[:, dl:dl + SC], qp2[0:128, 0:SC]), "sV")
                    qk_evac = qkv_evs[si % 2]
                wq_freeT[slot] = self.cnt["sT"]
                qkv_done = max(qkv_evs)

                colmm = {}
                rzv_q = {}

                def emit_colsum(q):
                    epar2 = q % 2
                    w("T", "sV", rzv_q[q])
                    if q == 0:
                        w("T", "sV", max(col_evacV, qkv_done))
                    for c in range(KCH):
                        bank, row = c // 3, 32 * (c % 3)
                        tc2 = op("T", lambda e, q2=q, b=bank, r=row, c2=c, ep=epar2:
                                 e.matmul(col_ps[b][r:r + 1, :], rzb[ep][:],
                                          exp_sb[ep][:, SC * c2:SC * (c2 + 1)],
                                          start=(q2 == 0), stop=(q2 == QT - 1)),
                                 "sT" if c == KCH - 1 else None)
                    ebuf_free[epar2] = tc2
                    colmm[q] = tc2

                for qt in range(QT):
                    epar = qt % 2
                    zc = (NEX + 2) * epar
                    last_exp = 0
                    for ch2 in range(NEX):
                        par = ch2 % 2
                        w("T", "sS", scps_free[par])
                        if ch2 == 0 and qt == 0:
                            w("T", "sV", qkv_done)
                        for j in range(EXPC // SC):
                            c512 = ch2 * (EXPC // SC) + j
                            tc = op("T", lambda e, q=qt, p=par, c=c512, jj=j:
                                    e.matmul(sc_ps[p][:, SC * jj:SC * (jj + 1)],
                                             qh[:, 128 * q:128 * (q + 1)],
                                             kh[:, SC * c:SC * (c + 1)],
                                             start=True, stop=True),
                                    "sT" if j == (EXPC // SC) - 1 else None)
                        w("S", "sT", tc)
                        w("S", "sT", ebuf_free[epar])
                        scps_free[par] = op(
                            "S", lambda e, p=par, c=ch2, ep=epar, z=zc:
                            e.activation(out=exp_sb[ep][:, EXPC * c:EXPC * (c + 1)],
                                         in_=sc_ps[p][:], func=AF.Exp, scale=DHSC,
                                         accum_out=zw[:, z + c:z + c + 1]), "sS")
                        last_exp = scps_free[par]
                    w("V", "sS", last_exp)
                    w("V", "sT", ebuf_free[epar])
                    vv = op("V", lambda e, z=zc: e.reduce_sum(zw[:, z + NEX:z + NEX + 1],
                                                              zw[:, z:z + NEX], axis=AXX), "sV")
                    w("V", "sV", vv)
                    vv = op("V", lambda e, z=zc: e.reciprocal(zw[:, z + NEX + 1:z + NEX + 2],
                                                              zw[:, z + NEX:z + NEX + 1]), "sV")
                    w("V", "sV", vv)
                    rzv_q[qt] = op("V", lambda e, z=zc, ep=epar:
                                   e.tensor_copy(rzb[ep][:], zw[:, z + NEX + 1:z + NEX + 2]), "sV")
                    if qt > 0:
                        emit_colsum(qt - 1)
                emit_colsum(QT - 1)
                prev_scoresT = self.cnt["sT"]
                w("V", "sT", colmm[QT - 1])
                for c in range(KCH):
                    bank, row = c // 3, 32 * (c % 3)
                    spar = c % 2
                    w("V", f"st{spar}", stg_dma[spar])
                    ev = op("V", lambda e, b=bank, r=row, sp=spar:
                            e.tensor_copy(stg[sp][:], col_ps[b][r:r + 1, :]), "sV")
                    w("S", "sV", ev)
                    stg_dma[spar] = op("S", lambda e, sp=spar, rr=c * H + h:
                                       e.dma_start(out=self.cc_in.ap()[rr:rr + 1, :],
                                                   in_=stg[sp][:]), f"st{spar}", 16)
                col_evacV = self.cnt["sV"]
            self.attn_dS = (self.cnt["st0"], self.cnt["st1"])
            self.barrier()
        self.barrier()
        self.ln1_oth_scope.close()

    def ph2b_vproj(self):
        cfg = self.cfg
        KT, OWN, H, OCH = cfg.KT, cfg.OWN, cfg.H, cfg.OCH
        w, op = self.w, self.op
        self.vsb_scope = contextlib.ExitStack()
        self.v_sb = self.sbuf(self.vsb_scope, "v_sb", [128, H, OWN], BF16)
        with contextlib.ExitStack() as ph:
            wv = [self.sbuf(ph, f"wv{i}", [128, KT, 128], BF16) for i in range(2)]
            v_ps = [self.psum(ph, f"v_ps{i}", [128, SC]) for i in range(2)]
            wv_dma = [0, 0]
            wv_freeT = [0, 0]
            vps_freeV = [0, 0]
            for h in range(H):
                slot = h % 2
                w("P", "sT", wv_freeT[slot])
                wv_dma[slot] = op("P", lambda e, hh=h, s=slot:
                                  e.dma_start(out=wv[s][:],
                                              in_=self.wqkh[hh].rearrange("a p f -> p a f")[:, :, 256:384]),
                                  f"wv{slot}", 16)
                for c in range(OCH):
                    par = (h * OCH + c) % 2
                    w("T", f"wv{slot}", wv_dma[slot])
                    w("T", "sV", vps_freeV[par])
                    for kt in range(KT):
                        tc = op("T", lambda e, s=slot, k=kt, c2=c, p=par:
                                e.matmul(v_ps[p][:], wv[s][:, k, :],
                                         self.ln1_own[:, k, SC * c2:SC * (c2 + 1)],
                                         start=(k == 0), stop=(k == KT - 1)),
                                "sT" if kt == KT - 1 else None)
                    w("V", "sT", tc)
                    vps_freeV[par] = op("V", lambda e, hh=h, c2=c, p=par:
                                        e.tensor_copy(self.v_sb[:, hh, SC * c2:SC * (c2 + 1)],
                                                      v_ps[p][:]), "sV")
                wv_freeT[slot] = self.cnt["sT"]
            self.barrier()
        self.barrier()
        self.ln1_own_scope.close()

    def ph3_val_wo(self):
        cfg = self.cfg
        KT, OWN, H, OCH, E = cfg.KT, cfg.OWN, cfg.H, cfg.OCH, cfg.E
        w, op = self.w, self.op
        v_sb = self.v_sb
        self.valT_scope = contextlib.ExitStack()
        valT = self.sbuf(self.valT_scope, "valT", [128, KT, OWN], BF16, side="left")
        gw_dY = op("Y", lambda e: e.dma_start(
            out=self.gwTb[:], in_=self.gwT_d.rearrange("(a p) f -> p a f", p=128)), "dG", 16)

        with contextlib.ExitStack() as ph:
            selR0 = self.sbuf(ph, "selR0", [cfg.R, OCH * H * 128], BF16)
            selR1 = self.sbuf(ph, "selR1", [cfg.R, OCH * H * 128], BF16)
            csb0 = self.sbuf(ph, "csb0", [cfg.R, SC], BF16)
            csb1 = self.sbuf(ph, "csb1", [cfg.R, SC], BF16)
            cbh = [self.sbuf(ph, f"cbh{i}", [128, OWN], F32) for i in range(2)]
            cb_ps = [self.psum(ph, f"cb_ps{i}", [128, SC]) for i in range(2)]
            sel_dY = 0
            for dst, src in ((selR0, self.selR0_d), (selR1, self.selR1_d)):
                sel_dY = op("Y", lambda e, d=dst, sr=src: e.dma_start(out=d[:], in_=sr), "dG", 16)
            w("P", "st0", self.attn_dS[0])
            w("P", "st1", self.attn_dS[1])
            import os as _os
            if _os.environ.get("NO_CC"):
                op("P", lambda e: e.dma_start(out=self.cc_out.ap()[0:cfg.R, :],
                                              in_=self.cc_in.ap()), "cc", 16)
                op("P", lambda e: e.dma_start(out=self.cc_out.ap()[cfg.R:2 * cfg.R, :],
                                              in_=self.cc_in.ap()), "cc", 16)
                w("P", "cc", self.cnt["cc"])
            else:
                self.streams["P"].append(
                    lambda e: e.collective_compute(
                        "AllGather", ALU.bypass,
                        replica_groups=[[0, 1], [2, 3], [4, 5], [6, 7]],
                        ins=[self.cc_in.ap().opt()],
                        outs=[self.cc_out.ap().opt()]).then_inc(self.sems["cc"]))
                self.cnt["cc"] = 1
            w("P", "cc", 1)
            csd = op("P", lambda e: e.dma_start(out=csb0[:], in_=self.cc_out.ap()[0:cfg.R, :]),
                     "cs0", 16)
            csd = op("P", lambda e: e.dma_start(out=csb1[:], in_=self.cc_out.ap()[cfg.R:2 * cfg.R, :]),
                     "cs0", 16)
            bc_free = [0, 0]
            bi = 0
            val_V = 0
            for h in range(H):
                hpar = h % 2
                for i in range(OCH):
                    blk = i * H + h
                    par = bi % 2
                    w("T", "cs0", csd)
                    w("T", "dG", sel_dY)
                    w("T", "sV", bc_free[par])
                    op("T", lambda e, b=blk, p=par:
                       e.matmul(cb_ps[p][:], selR0[:, 128 * b:128 * (b + 1)], csb0[:],
                                start=True, stop=False), None)
                    tb = op("T", lambda e, b=blk, p=par:
                            e.matmul(cb_ps[p][:], selR1[:, 128 * b:128 * (b + 1)], csb1[:],
                                     start=False, stop=True), "sT")
                    w("V", "sT", tb)
                    w("V", "sV", val_V if i == 0 else 0)
                    bc_free[par] = op("V", lambda e, hp=hpar, i2=i, p=par:
                                      e.tensor_copy(cbh[hp][:, SC * i2:SC * (i2 + 1)], cb_ps[p][:]),
                                      "sV")
                    bi += 1
                w("V", "sV", bc_free[(bi - 1) % 2])
                val_V = op("V", lambda e, hh=h, hp=hpar:
                           e.tensor_mul(valT[:, hh, :], v_sb[:, hh, :], cbh[hp][:]), "sV")
            self.barrier()
        self.vsb_scope.close()
        if self.dump == "attn":
            self.dmp("valT", [KT * 128, OWN])
            self.dump_tiled("valT", valT)

        with contextlib.ExitStack() as ph:
            woh = [self.sbuf(ph, f"woh{i}", [128, KT, 128], BF16) for i in range(2)]
            xr = [self.sbuf(ph, f"xr{i}", [128, SC], F32) for i in range(2)]
            xn = [self.sbuf(ph, f"xn{i}", [128, SC], F32) for i in range(2)]
            xnsq = [self.sbuf(ph, f"xnsq{i}", [128, SC], F32) for i in range(2)]
            l0sb = self.sbuf(ph, "l0sb", [E, OWN], F32)
            self.l0sb = l0sb
            wo_ps = [self.psum(ph, f"wo_ps{i}", [128, SC]) for i in range(2)]
            l0_ps = [self.psum(ph, f"l0_ps{i}", [E, SC]) for i in range(OCH)]
            m2_ps = self.psum(ph, "m2_ps", [8, SC])
            q2_ps = self.psum(ph, "q2_ps", [8, SC])

            wo_dma = [0, 0]
            wo_freeT = [0, 0]
            xn_freeV = [0, 0]
            xn_lastT = [0, 0]
            xn_lastS = [0, 0]
            xn_lastY = [0, 0]
            i2 = 0
            total = KT * OCH
            for dt_i in range(KT):
                s2 = dt_i % 2
                w("P", "sT", wo_freeT[s2])
                wo_dma[s2] = op("P", lambda e, d2=dt_i, s3=s2:
                                e.dma_start(out=woh[s3][:],
                                            in_=self.wo_d.rearrange("(a p) f -> p a f", p=128)
                                            [:, :, 128 * d2:128 * (d2 + 1)]),
                                f"wo{s2}", 16)
                for c in range(OCH):
                    par = i2 % 2
                    w("T", f"wo{s2}", wo_dma[s2])
                    w("T", "dG", self.cnt["dG"])
                    w("T", "sV", xn_freeV[par])
                    for kt in range(KT):
                        tc = op("T", lambda e, k=kt, c2=c, p=par, s3=s2:
                                e.matmul(wo_ps[p][:], woh[s3][:, k, :],
                                         valT[:, k, SC * c2:SC * (c2 + 1)],
                                         start=(k == 0), stop=(k == KT - 1)),
                                "sT" if kt == KT - 1 else None)
                    if c == OCH - 1:
                        wo_freeT[s2] = self.cnt["sT"]
                    dyx = op("Y", lambda e, d2=dt_i, c2=c, p=par:
                             e.dma_start(out=xr[p][:],
                                         in_=self.xT[128 * d2:128 * (d2 + 1), SC * c2:SC * (c2 + 1)]),
                             f"xr{par}", 16)
                    w("V", "sT", tc)
                    w("V", f"xr{par}", dyx)
                    w("V", "sT", xn_lastT[par])
                    w("V", "sS", xn_lastS[par])
                    w("V", f"xo{par}", xn_lastY[par])
                    vv = op("V", lambda e, p=par: e.tensor_add(xn[p][:], xr[p][:], wo_ps[p][:]), "sV")
                    xn_freeV[par] = vv
                    w("Y", "sV", vv)
                    xn_lastY[par] = op("Y", lambda e, d2=dt_i, c2=c, p=par:
                                       e.dma_start(out=self.xnew_d[128 * d2:128 * (d2 + 1),
                                                                   SC * c2:SC * (c2 + 1)],
                                                   in_=xn[p][:]), f"xo{par}", 16)
                    w("S", "sV", vv)
                    ssq = op("S", lambda e, p=par:
                             e.activation(out=xnsq[p][:], in_=xn[p][:], func=AF.Square), "sS")
                    xn_lastS[par] = ssq
                    first = (i2 == 0)
                    last = (i2 == total - 1)
                    w("T", "sV", vv)
                    op("T", lambda e, c2=c, p=par, f=first, l=last:
                       e.matmul(m2_ps[:], self.oh8[:, 8 * c2:8 * (c2 + 1)], xn[p][:],
                                start=f, stop=l), None)
                    w("T", "sS", ssq)
                    op("T", lambda e, c2=c, p=par, f=first, l=last:
                       e.matmul(q2_ps[:], self.oh8[:, 8 * c2:8 * (c2 + 1)], xnsq[p][:],
                                start=f, stop=l), None)
                    tgl = op("T", lambda e, d2=dt_i, c2=c, p=par:
                             e.matmul(l0_ps[c2][:], self.gwTb[:, d2, :], xn[p][:],
                                      start=(d2 == 0), stop=(d2 == KT - 1)), "sT")
                    xn_lastT[par] = tgl
                    i2 += 1
            stats2_T = self.cnt["sT"]
            w("T", "sV", xn_freeV[0])
            w("T", "sV", xn_freeV[1])
            for kt in range(KT):
                tgs = op("T", lambda e, k=kt:
                         e.matmul(wo_ps[0][0:1, 0:E], self.ones128[:], self.gwTb[:, k, :],
                                  start=(k == 0), stop=(k == KT - 1)),
                         "sT" if kt == KT - 1 else None)
            w("V", "sT", tgs)
            gse = op("V", lambda e: e.tensor_copy(self.gsrow[:], wo_ps[0][0:1, 0:E]), "sV")
            w("T", "sV", gse)
            tb = op("T", lambda e: e.matmul(wo_ps[1][:, 0:E], self.sel8[0:1, 0:128], self.gsrow[:],
                                            start=True, stop=True), "sT")
            w("V", "sT", tb)
            gsb_V = op("V", lambda e: e.tensor_copy(self.gsum_b[:], wo_ps[1][:, 0:E]), "sV")
            w("V", "sT", stats2_T)
            op("V", lambda e: e.tensor_copy(self.nm2_row[:], m2_ps[:]), "sV")
            vv = op("V", lambda e: e.tensor_copy(self.r2w[:], q2_ps[:]), "sV")
            w("V", "sV", vv)
            vv = op("V", lambda e: e.tensor_mul(self.rstd2_row[:], self.nm2_row[:], self.nm2_row[:]), "sV")
            w("V", "sV", vv)
            vv = op("V", lambda e: e.tensor_sub(self.r2w[:], self.r2w[:], self.rstd2_row[:]), "sV")
            w("S", "sV", vv)
            ssq = op("S", lambda e: e.activation(out=self.r2w[:], in_=self.r2w[:],
                                                 func=AF.Sqrt, bias=self.epsb[0:8, :]), "sS")
            w("V", "sS", ssq)
            vv = op("V", lambda e: e.reciprocal(self.rstd2_row[:], self.r2w[:]), "sV")
            w("V", "sV", vv)
            nm2_V = op("V", lambda e: e.tensor_scalar_mul(self.nm2_row[:], self.nm2_row[:], -1.0), "sV")
            trf = gsb_V
            njj = min(OWN, SC) // 128
            for jj in range(njj):
                for src, dst in ((self.nm2_row, self.nm2T), (self.rstd2_row, self.rstd2T)):
                    w("T", "sV", max(nm2_V, trf))
                    tt2 = op("T", lambda e, j=jj, sr=src:
                             e.transpose(wo_ps[0][:, 0:8], sr[:, 128 * j:128 * (j + 1)],
                                         self.idf[0:8, 0:8]), "sT")
                    w("V", "sT", tt2)
                    trf = op("V", lambda e, j=jj, d=dst:
                             e.tensor_copy(d[:, j, :], wo_ps[0][:, 0:8]), "sV")
            l0ev = 0
            for c in range(OCH):
                w("V", "sT", stats2_T)
                l0ev = op("V", lambda e, c2=c:
                          e.tensor_copy(self.l0sb[:, SC * c2:SC * (c2 + 1)], l0_ps[c2][:]), "sV")
            trf2 = trf
            for tt in range(cfg.TT):
                w("T", "sV", max(l0ev, trf2))
                tt2 = op("T", lambda e, t=tt:
                         e.transpose(wo_ps[1][:, 0:E], self.l0sb[:, 128 * t:128 * (t + 1)],
                                     self.idf[0:E, 0:E]), "sT")
                w("V", "sT", tt2)
                trf2 = op("V", lambda e, t=tt:
                          e.tensor_copy(self.l0T[:, t, :], wo_ps[1][:, 0:E]), "sV")
            self.l0T_V = self.cnt["sV"]
            self.barrier()
        self.barrier()
        self.valT_scope.close()
        if self.dump == "attn":
            self.dmp("xnew", [cfg.D, OWN])
            op("Y", lambda e: e.dma_start(out=self.dumps["xnew"], in_=self.xnew_d), "dC", 16)

    def ph4_route(self):
        cfg = self.cfg
        E, CH, TT, OWN = cfg.E, cfg.CH, cfg.TT, cfg.OWN
        w, op = self.w, self.op
        with contextlib.ExitStack() as ph:
            lg = self.sbuf(ph, "lg", [128, E], F32)
            wk = self.sbuf(ph, "wk", [128, E], F32)
            msk = self.sbuf(ph, "msk", [128, E], F32)
            esb = self.sbuf(ph, "esb", [128, E], F32)
            cwt = self.sbuf(ph, "cwt", [128, TT, E], F32)
            cws = self.sbuf(ph, "cws", [128, TT, CH], F32)
            mx = self.sbuf(ph, "mx", [128, 6], F32)
            tr_ps = self.psum(ph, "tr_ps", [128, 128])

            prevS = 0
            vv = 0
            for tt in range(TT):
                c2, jj = (128 * tt) // SC, ((128 * tt) % SC) // 128
                w("V", "sV", self.l0T_V)
                w("V", "sS", prevS)
                w("V", "sV", vv)
                vv = op("V", lambda e, t=tt, c=c2, j=jj:
                        e.scalar_tensor_tensor(lg[:], self.gsum_b[:],
                                               self.nm2T[:, j, c:c + 1], self.l0T[:, t, :],
                                               ALU.mult, ALU.add), "sV")
                w("V", "sV", vv)
                vv = op("V", lambda e, c=c2, j=jj:
                        e.tensor_scalar(lg[:], lg[:], self.rstd2T[:, j, c:c + 1], None,
                                        ALU.mult), "sV")
                w("V", "sV", vv)
                vv = op("V", lambda e: e.reduce_max(mx[:, 0:1], lg[:], axis=AXX), "sV")
                w("V", "sV", vv)
                vv = op("V", lambda e: e.tensor_copy(wk[:], lg[:]), "sV")
                for it in range(3):
                    w("V", "sV", vv)
                    vv = op("V", lambda e, i=it: e.tensor_scalar(msk[:], wk[:], mx[:, i:i + 1],
                                                                 None, ALU.is_ge), "sV")
                    w("V", "sV", vv)
                    vv = op("V", lambda e: e.scalar_tensor_tensor(wk[:], msk[:], -1e30, wk[:],
                                                                  ALU.mult, ALU.add), "sV")
                    w("V", "sV", vv)
                    vv = op("V", lambda e, i=it: e.reduce_max(mx[:, i + 1:i + 2], wk[:], axis=AXX), "sV")
                w("V", "sV", vv)
                vv = op("V", lambda e: e.tensor_scalar(msk[:], lg[:], mx[:, 3:4], None, ALU.is_ge), "sV")
                w("V", "sV", vv)
                vv = op("V", lambda e: e.tensor_scalar_mul(mx[:, 4:5], mx[:, 0:1], -1.0), "sV")
                w("S", "sV", vv)
                prevS = op("S", lambda e: e.activation(out=esb[:], in_=lg[:], func=AF.Exp,
                                                       bias=mx[:, 4:5]), "sS")
                w("V", "sS", prevS)
                vv = op("V", lambda e: e.tensor_mul(esb[:], esb[:], msk[:]), "sV")
                w("V", "sV", vv)
                vv = op("V", lambda e: e.reduce_sum(mx[:, 5:6], esb[:], axis=AXX), "sV")
                w("V", "sV", vv)
                vv = op("V", lambda e: e.reciprocal(mx[:, 5:6], mx[:, 5:6]), "sV")
                w("V", "sV", vv)
                vv = op("V", lambda e, t=tt: e.tensor_scalar(cwt[:, t, :], esb[:], mx[:, 5:6],
                                                             None, ALU.mult), "sV")
                w("V", "sV", vv)
                vv = op("V", lambda e, t=tt: e.tensor_add(cws[:, t, :], cwt[:, t, 0:E:2],
                                                          cwt[:, t, 1:E:2]), "sV")
            route_V = self.cnt["sV"]
            trf = 0
            for src, dst, n in ((cwt, self.cwlT, E), (cws, self.cwsT, CH)):
                for tt in range(TT):
                    w("T", "sV", max(route_V, trf))
                    t1 = op("T", lambda e, t=tt, sr=src, nn=n:
                            e.transpose(tr_ps[0:nn, 0:128], sr[:, t, :], self.idf[:]), "sT")
                    w("V", "sT", t1)
                    trf = op("V", lambda e, t=tt, d=dst, nn=n:
                             e.tensor_copy(d[:, 128 * t:128 * (t + 1)], tr_ps[0:nn, 0:128]), "sV")
            self.barrier()
        self.barrier()
        if self.dump == "route":
            self.dmp("cwl", [E, OWN])
            self.dmp("cws", [CH, OWN])
            self.dump_flat("cwl", self.cwlT)
            self.dump_flat("cws", self.cwsT)

    def ph5_ffn(self):
        cfg = self.cfg
        KT, OWN, E, CH, IT, CIT, QL, NQ = (cfg.KT, cfg.OWN, cfg.E, cfg.CH,
                                           cfg.IT, cfg.CIT, cfg.QL, cfg.NQ)
        INTER, CI, D = cfg.INTER, cfg.CI, cfg.D
        w, op = self.w, self.op
        with contextlib.ExitStack() as ph:
            selsl = [self.sbuf(ph, f"selsl{i}", [E, 128], BF16) for i in range(2)]
            ln2q = self.sbuf(ph, "ln2q", [128, KT, QL], BF16)
            acc = self.sbuf(ph, "acc", [128, KT, QL], F32)
            xq = [self.sbuf(ph, f"xq{i}", [128, QL], F32) for i in range(4)]
            t2 = [self.sbuf(ph, f"t2{i}", [128, QL], F32) for i in range(2)]
            nmq = self.sbuf(ph, "nmq", [128, QL], F32)
            rsq = self.sbuf(ph, "rsq", [128, QL], F32)
            wgs = self.sbuf(ph, "wgs", [128, KT, INTER], BF16)
            wus = self.sbuf(ph, "wus", [128, KT, INTER], BF16)
            wds = self.sbuf(ph, "wds", [128, IT, D], BF16)
            cgs = self.sbuf(ph, "cgs", [128, KT, CI], BF16)
            cus = self.sbuf(ph, "cus", [128, KT, CI], BF16)
            cds = self.sbuf(ph, "cds", [128, CIT, D], BF16)
            sgs = [self.sbuf(ph, "sgs0", [128, IT, QL], BF16)] * 2
            ums = [self.sbuf(ph, "ums0", [128, IT, QL], BF16)] * 2
            asb = [self.sbuf(ph, f"asb{i}", [128, IT, QL], BF16) for i in range(2)]
            cwb = [self.sbuf(ph, f"cwb{i}", [128, QL], BF16) for i in range(2)]
            gu_ps = [self.psum(ph, f"gu_ps{i}", [128, QL]) for i in range(4)]
            d_ps = [self.psum(ph, f"d_ps{i}", [128, QL]) for i in range(2)]
            cw_ps = [self.psum(ph, f"cw_ps{i}", [128, QL]) for i in range(2)]

            state = {
                "slot_freeT": {},
                "gu_freeS": [0] * 4, "gu_freeV": [0] * 4,
                "d_freeV": [0, 0], "cw_freeV": [0, 0],
                "sg_freeP": [0, 0], "um_freeP": [0, 0], "a_freeT": [0, 0],
                "xq_freeV": [0, 0, 0, 0], "acc_dY": 0, "ln2q_P": 0, "gu_lastT": 0,
                "cwb_V": [0, 0],
            }

            for qp in range(NQ):
                q0 = qp * QL
                nbc = QL // 128
                bi2 = 0
                lastb = [state["cw_freeV"][0], state["cw_freeV"][1]]
                for j in range(nbc):
                    t0 = q0 + 128 * j
                    c2, jj = t0 // SC, (t0 % SC) // 128
                    for src, dst in ((self.nm2_row, nmq), (self.rstd2_row, rsq)):
                        bp = bi2 % 2
                        w("T", "sV", lastb[bp])
                        tb = op("T", lambda e, c=c2, j2=jj, sr=src, p=bp:
                                e.matmul(cw_ps[p][:, 0:128],
                                         self.sel8[:, 128 * c:128 * (c + 1)],
                                         sr[:, 128 * j2:128 * (j2 + 1)],
                                         start=True, stop=True), "sT")
                        w("V", "sT", tb)
                        if j == 0 and src is self.nm2_row and qp > 0:
                            w("V", "sP", state["ln2q_P"])
                        lastb[bp] = op("V", lambda e, d=dst, j2=j, p=bp:
                                       e.tensor_copy(d[:, 128 * j2:128 * (j2 + 1)],
                                                     cw_ps[p][:, 0:128]), "sV")
                        bi2 += 1
                state["cw_freeV"][0] = lastb[0]
                state["cw_freeV"][1] = max(state["cw_freeV"][1], lastb[1])
                bc_V = self.cnt["sV"]
                lastP = 0
                t2_freeP = state.setdefault("t2_freeP", [0, 0])
                state.setdefault("xq_freeV", [0, 0, 0, 0])
                for kt in range(KT):
                    par = kt % 4
                    w("Y", "sV", state["xq_freeV"][par])
                    dy = op("Y", lambda e, k=kt, p=par, q=q0:
                            e.dma_start(out=xq[p][:],
                                        in_=self.xnew_d[128 * k:128 * (k + 1), q:q + QL]),
                            f"xq{par}", 16)
                    w("V", f"xq{par}", dy)
                    tpar = kt % 2
                    if kt < 2:
                        w("V", "sV", bc_V)
                        w("V", "sT", state["gu_lastT"])
                    w("V", "sP", t2_freeP[tpar])
                    vv = op("V", lambda e, p=par, tp=tpar:
                            e.tensor_mul(t2[tp][:], xq[p][:], rsq[:]), "sV")
                    state["xq_freeV"][par] = vv
                    w("P", "sV", vv)
                    lastP = op("P", lambda e, k=kt, tp=tpar:
                               e.tensor_add(ln2q[:, k, :], t2[tp][:], nmq[:]), "sP")
                    t2_freeP[tpar] = lastP
                state["ln2q_P"] = lastP
                w("Y", "ao", state["acc_dY"])
                state["seed_dY"] = op("Y", lambda e, q=q0: e.dma_start(
                    out=acc[:],
                    in_=self.xnew_d.rearrange("(a p) f -> p a f", p=128)[:, :, q:q + QL]),
                    "ao", 16)

                def expert(idx, wgd, wud, wdd, wgt, wut, wdt, sg, um, aa, cwT,
                           selT, nIT, kind, scale):
                    st = state
                    apar = idx % 2
                    silu_oi = {}
                    umv_oi = {}
                    for nm, wt2, wdram in ((kind + "g", wgt, wgd), (kind + "u", wut, wud),
                                           (kind + "d", wdt, wdd)):
                        w("P", "sT", st["slot_freeT"].get(nm, 0))
                        st["slot_freeT"][nm + "_dma"] = op(
                            "P", lambda e, i=idx, t3=wt2, dr=wdram:
                            e.dma_start(out=t3[:], in_=dr[i].rearrange("(a p) f -> p a f", p=128)),
                            nm, 16)
                    dg = st["slot_freeT"][kind + "g_dma"]
                    du = st["slot_freeT"][kind + "u_dma"]
                    dd = st["slot_freeT"][kind + "d_dma"]
                    ssl = st.setdefault("selslot", 0)
                    st["selslot"] = 1 - ssl
                    w("S", "sT", st.setdefault("sel_freeT", [0, 0])[ssl])
                    seld = op("S", lambda e, i=idx, sp=ssl, sd=selT, nn=(E if kind == "L" else CH):
                              e.dma_start(out=selsl[sp][0:nn, :], in_=sd[:, 128 * i:128 * (i + 1)]),
                              f"se{ssl}", 16)
                    w("T", "sV", st["cw_freeV"][apar])
                    w("T", f"se{ssl}", seld)
                    tb = op("T", lambda e, p=apar, ct=cwT, sl=selsl[ssl],
                            nn=(E if kind == "L" else CH), q=q0:
                            e.matmul(cw_ps[p][:], sl[0:nn, :],
                                     ct[:, q:q + QL], start=True, stop=True), "sT")
                    st["sel_freeT"][ssl] = self.cnt["sT"]
                    w("V", "sT", tb)
                    w("V", "sV", st["cwb_V"][apar])
                    cwv = op("V", lambda e, p=apar, sc2=scale:
                             e.tensor_scalar_mul(cwb[p][:], cw_ps[p][:], sc2), "sV")
                    st["cw_freeV"][apar] = cwv
                    for gu, wt2, dmac in ((0, wgt, dg), (1, wut, du)):
                        for oi in range(nIT):
                            pp = 2 * gu + (oi % 2)
                            w("T", kind + ("g" if gu == 0 else "u"), dmac)
                            if gu == 0:
                                w("T", "sS", st["gu_freeS"][pp])
                            else:
                                w("T", "sV", st["gu_freeV"][pp])
                            w("T", "sP", st["ln2q_P"])
                            for kt in range(KT):
                                tc = op("T", lambda e, k=kt, o=oi, w2=wt2, p=pp:
                                        e.matmul(gu_ps[p][:], w2[:, k, 128 * o:128 * (o + 1)],
                                                 ln2q[:, k, :], start=(k == 0), stop=(k == KT - 1)),
                                        "sT" if kt == KT - 1 else None)
                            if gu == 0 and oi == nIT - 1:
                                st["slot_freeT"][kind + "g"] = tc
                            if gu == 1 and oi == nIT - 1:
                                st["slot_freeT"][kind + "u"] = tc
                            if gu == 0:
                                w("S", "sT", tc)
                                w("S", "sP", st["sg_freeP"][apar])
                                st["gu_freeS"][pp] = op(
                                    "S", lambda e, o=oi, p=pp, a2=apar, sg2=sg:
                                    e.activation(out=sg2[a2][:, o, :], in_=gu_ps[p][:],
                                                 func=AF.Silu), "sS")
                            else:
                                w("V", "sT", tc)
                                w("V", "sP", st["um_freeP"][apar])
                                w("V", "sV", cwv)
                                st["gu_freeV"][pp] = op(
                                    "V", lambda e, o=oi, p=pp, a2=apar, um2=um:
                                    e.tensor_mul(um2[a2][:, o, :], gu_ps[p][:], cwb[a2][:]), "sV")
                                umv_oi[oi] = st["gu_freeV"][pp]
                            if gu == 0:
                                silu_oi[oi] = st["gu_freeS"][pp]
                    st["cwb_V"][apar] = self.cnt["sV"]
                    pa = 0
                    for oi in range(nIT):
                        w("P", "sS", silu_oi[oi])
                        w("P", "sV", umv_oi[oi])
                        if oi == 0:
                            w("P", "sT", st["a_freeT"][apar])
                        pa = op("P", lambda e, o=oi, a2=apar, sg2=sg, um2=um, aa2=aa:
                                e.tensor_mul(aa2[a2][:, o, :], sg2[a2][:, o, :], um2[a2][:, o, :]),
                                "sP")
                    st["sg_freeP"][0] = st["sg_freeP"][1] = pa
                    st["um_freeP"][0] = st["um_freeP"][1] = pa
                    w("T", "sP", pa)
                    w("T", kind + "d", dd)
                    for oi in range(KT):
                        dp = oi % 2
                        w("T", "sV", st["d_freeV"][dp])
                        for it2 in range(nIT):
                            tc = op("T", lambda e, o=oi, i3=it2, p=dp, a2=apar, w3=wdt, aa2=aa:
                                    e.matmul(d_ps[p][:], w3[:, i3, 128 * o:128 * (o + 1)],
                                             aa2[a2][:, i3, :], start=(i3 == 0), stop=(i3 == nIT - 1)),
                                    "sT" if it2 == nIT - 1 else None)
                        if oi == KT - 1:
                            st["a_freeT"][apar] = tc
                            st["slot_freeT"][kind + "d"] = tc
                        w("V", "sT", tc)
                        if oi < 2 and idx == 0 and kind == "L":
                            w("V", "ao", state["seed_dY"])
                        st["d_freeV"][dp] = op("V", lambda e, o=oi, p=dp:
                                               e.tensor_add(acc[:, o, :], acc[:, o, :], d_ps[p][:]),
                                               "sV")
                    state["gu_lastT"] = max(state["gu_lastT"],
                                            st["slot_freeT"][kind + "g"],
                                            st["slot_freeT"][kind + "u"])

                for e_i in range(E):
                    expert(e_i, self.wg_d, self.wu_d, self.wd_d, wgs, wus, wds,
                           sgs, ums, asb, self.cwlT, self.selE_d, IT, "L", 1.0)
                for e_i in range(CH):
                    expert(e_i, self.cwg_d, self.cwu_d, self.cwd_d, cgs, cus, cds,
                           sgs, ums, asb, self.cwsT, self.selC_d, CIT, "S", 0.05)
                ffn_V = self.cnt["sV"]
                w("Y", "sV", ffn_V)
                state["acc_dY"] = op("Y", lambda e, q=q0: e.dma_start(
                    out=self.out_d.rearrange("(a p) f -> p a f", p=128)[:, :, q:q + QL],
                    in_=acc[:]), "ao", 16)
            self.barrier()

    def finish(self):
        for nm in ("valT_scope", "vsb_scope", "ln1_oth_scope", "ln1_own_scope"):
            scp = getattr(self, nm, None)
            if scp is not None:
                scp.close()
                setattr(self, nm, None)
        for s, v in self.cnt.items():
            self.w("Y", s, v)
        streams = self.streams
        with self.nc.Block() as block:
            block.tensor(lambda e: [f(e) for f in streams["T"]])
            block.scalar(lambda e: [f(e) for f in streams["S"]])
            block.vector(lambda e: [f(e) for f in streams["V"]])
            block.gpsimd(lambda e: [f(e) for f in streams["P"]])
            block.sync(lambda e: [f(e) for f in streams["Y"]])
        self.es.close()
        return self.nc


def build_program(cfg, dump=None):
    b = Builder(cfg, dump)
    b.declare_io()
    b.load_consts()
    b.ph1_ln1()
    if dump == "ln1":
        return b.finish()
    b.ph2_attn()
    b.ph2b_vproj()
    b.ph3_val_wo()
    if dump == "attn":
        return b.finish()
    b.ph4_route()
    if dump == "route":
        return b.finish()
    b.ph5_ffn()
    return b.finish()


def layout_inputs(cfg, inputs, n_cores=8):
    x = np.asarray(inputs["L0_attn_x"])
    wqkv = np.asarray(inputs["L0_attn_w_qkv"])
    wo = np.asarray(inputs["L0_attn_w_o"])
    gate = np.asarray(inputs["L0_ffn_gate_w"])
    consts = mk_consts(cfg)
    H, KT, OWN = cfg.H, cfg.KT, cfg.OWN
    wq2 = wqkv.reshape(KT, 128, 3, H, cfg.DH)
    wqkh = np.ascontiguousarray(
        np.transpose(wq2, (3, 0, 1, 2, 4)).reshape(H, KT, 128, 3 * cfg.DH))
    gwT = np.ascontiguousarray(gate.T)
    selRs = [mk_selR(cfg, hf) for hf in (0, 1)]
    in_maps = []
    for c in range(n_cores):
        b_i, half = c // 2, c % 2
        xTb = x[b_i].T
        own = xTb[:, half * OWN:(half + 1) * OWN]
        oth = xTb[:, (1 - half) * OWN:(2 - half) * OWN]
        m = {
            "xT": np.ascontiguousarray(np.concatenate([own, oth], axis=1)),
            "wqkh": wqkh,
            "wo": wo,
            "gwT": gwT,
            "wg": np.asarray(inputs["L0_ffn_Wg"]),
            "wu": np.asarray(inputs["L0_ffn_Wu"]),
            "wd": np.asarray(inputs["L0_ffn_Wd"]),
            "cwg": np.asarray(inputs["L0_ffn_cWg"]),
            "cwu": np.asarray(inputs["L0_ffn_cWu"]),
            "cwd": np.asarray(inputs["L0_ffn_cWd"]),
            "selR0": selRs[half][0],
            "selR1": selRs[half][1],
        }
        for k in ("idf", "oh8", "sel8", "ones128", "epsb", "selE", "selC"):
            m[k] = consts[k]
        in_maps.append(m)
    return in_maps


_CACHE = {}


def kernel(**inputs):
    cfg = FULL
    in_maps = layout_inputs(cfg, inputs)
    if "nc" not in _CACHE:
        _CACHE["nc"] = build_program(cfg)
    res = run_bass_kernel_spmd(_CACHE["nc"], in_maps, core_ids=list(range(8)))
    B, S, D, OWN = 4, 4096, cfg.D, cfg.OWN
    out = np.empty((B, S, D), dtype=np.float32)
    for c in range(8):
        b_i, half = c // 2, c % 2
        out[b_i, half * OWN:(half + 1) * OWN, :] = res.results[c]["out"].T
    return out
